# version 08
# baseline (speedup 1.0000x reference)
import numpy as np
import ml_dtypes

import concourse.bass as bass
import concourse.mybir as mybir
from concourse.bass_utils import run_bass_kernel_spmd

F32 = mybir.dt.float32
BF16 = mybir.dt.bfloat16
AF = mybir.ActivationFunctionType
ALU = mybir.AluOpType
AX = mybir.AxisListType
BF = ml_dtypes.bfloat16

D = 2048
DC = 16
TSEQ = 4096
TOWN = 1024
NHL = 4
E = 16
DI = 1024
IC = 8
SM_SCALE = 1.0 / float(np.sqrt(128.0))
SM_SHIFT = -24.0
NEG_BIG = -1.0e9
LN_EPS = 1e-5
RT_EPS = 1e-9
SUM_EPS = 1e-30

ENGS = ("sync", "gpsimd", "tensor", "scalar", "vector")


class Plan:
    def __init__(self, nc):
        self.nc = nc
        self.ops = {e: [] for e in ENGS}
        self.cnt = {}
        self.sems = {}

    def sem(self, name):
        if name not in self.sems:
            self.sems[name] = self.nc.alloc_semaphore(name)
            self.cnt[name] = 0
        return self.sems[name]

    def emit(self, eng, fn, waits=(), inc=None, add=1, raw=False):
        val = None
        if inc is not None:
            self.sem(inc)
            self.cnt[inc] += add
            val = self.cnt[inc]
        w = tuple((s, v) for (s, v) in waits if v is not None and v > 0)
        for s, _ in w:
            self.sem(s)
        self.ops[eng].append((fn, w, inc, add, raw))
        return val

    def dma(self, qeng, fn, waits=(), sem=None):
        return self.emit(qeng, fn, waits, inc=sem, add=16)

    def replay(self, eng_obj, name):
        hw = {}
        for fn, waits, inc, add, raw in self.ops[name]:
            for s, v in waits:
                if hw.get(s, 0) < v:
                    eng_obj.wait_ge(self.sems[s], v)
                    hw[s] = v
            if raw:
                fn(eng_obj, self)
            else:
                ins = fn(eng_obj)
                if inc is not None:
                    ins.then_inc(self.sems[inc], add)


def build_program(taps=None):
    taps = set(taps or ())
    nc = bass.Bass("TRN2", target_bir_lowering=False, debug=False)
    P = Plan(nc)

    def dri(n, s, dt=F32):
        return nc.dram_tensor(n, s, dt, kind="ExternalInput").ap()

    x_seq = dri("x_seq", [TSEQ, D])
    x_own = dri("x_own", [TOWN, D])
    wqkv = dri("wqkv", [D, 1536])
    bqt_in = dri("bqt_in", [128, 8])
    bv_row = dri("bv_row", [1, 512])
    w_proj = dri("w_proj", [D, D])
    bproj_row = dri("bproj_row", [1, D])
    wr_t = dri("wr_t", [D, E])
    rbias_row = dri("rbias_row", [1, E])
    Wg_in = dri("Wg_in", [E, D, DI])
    Wu_in = dri("Wu_in", [E, D, DI])
    Wd_in = dri("Wd_in", [E, DI, D])
    sWg_in = dri("sWg_in", [D, DI])
    sWu_in = dri("sWu_in", [D, DI])
    sWd_in = dri("sWd_in", [DI, D])

    out_own = nc.dram_tensor("out_own", [TOWN, D], F32, kind="ExternalOutput").ap()

    ag_in = nc.dram_tensor("ag_in", [512, TSEQ], BF16, kind="Internal")
    ag_outh = [nc.dram_tensor("ag_outh%d" % l, [512, TSEQ], BF16, kind="Internal")
               for l in range(NHL)]
    xn_all = nc.dram_tensor("xn_all", [8, 128, DC, 512], BF16, kind="Internal")

    ident_dr = nc.inline_tensor(np.eye(128, dtype=BF), name="c_ident")
    mk = np.zeros((128, 4, 512), dtype=np.float32)
    for o in range(4):
        kkk = np.arange(128)[:, None] + 128 * o
        qqq = np.arange(512)[None, :]
        mk[:, o, :] = np.where(kkk > qqq, NEG_BIG, 0.0)
    masks_dr = nc.inline_tensor(mk.astype(BF).reshape(128, 4 * 512), name="c_masks")
    ones_k_dr = nc.inline_tensor(np.ones((128, 1), dtype=BF), name="c_onesk")
    ones_r_dr = nc.inline_tensor(np.ones((1, 128), dtype=np.float32), name="c_onesr")
    oh = np.zeros((16, 16, 128), dtype=np.float32)
    for ee in range(16):
        oh[ee, ee, :] = 1.0
    onehot_dr = nc.inline_tensor(oh.astype(BF).reshape(16, 16 * 128), name="c_onehot")

    LO, HI = 18432, 229344
    S0 = LO
    S1 = LO + 98304
    S2 = S1 + 49152
    S3 = S2 + 49152

    def at(name, shape, dtype, off):
        esz = 2 if dtype == BF16 else 4
        n = 1
        for s in shape[1:]:
            n *= s
        assert off + n * esz <= HI, (name, off, n * esz)
        return nc.alloc_sbuf_tensor_at(name, shape, dtype, offset=off)

    q_sb = at("q_sb", [128, NHL, TSEQ], BF16, S0)
    k_sb = at("k_sb", [128, NHL, TSEQ], BF16, S0 + 32768)
    v_sb = at("v_sb", [128, 32, 512], BF16, S0 + 65536)
    acc = at("acc", [128, 8, D], F32, S0)
    a2 = at("a2", [128, 2, IC, TOWN], BF16, S0 + 65536)
    yown = at("yown", [128, DC, TOWN], BF16, S0 + 65536)

    wqkv_sb = at("wqkv_sb", [128, DC, 12, 128], BF16, S1)
    masks_sb = at("masks_sb", [128, 4, 512], BF16, S1)
    bprojb = at("bprojb", [128, D], F32, S1 + 4096)
    ptile = at("ptile", [128, 4, 512], BF16, S1 + 12288)
    ynorm = at("ynorm", [128, 2, 512], BF16, S1 + 16384)
    inv_sb = at("inv_sb", [1, 2, 512], F32, S1 + 18432)
    sumtmp = at("sumtmp", [1, 2, 512], F32, S1 + 22528)
    invb_ev = at("invb_ev", [128, 2, 512], F32, S1 + 26624)
    xo3 = at("xo3", [128, 2, D], BF16, S1 + 12288)
    xnt_b = at("xnt_b", [128, 2, D], BF16, S1 + 20480)
    wdr = at("wdr", [128, 3, IC, 512], BF16, S1 + 12288)
    wproj_b = at("wproj_b", [128, 4, D], BF16, S1 + 12288)
    wT_sb = at("wT_sb", [16, 1024], BF16, S1 + 36864)
    onehot_sb = at("onehot_sb", [16, 16 * 128], BF16, S1 + 38912)
    wbc_sb = at("wbc_sb", [128, 2, 1024], BF16, S1 + 43008)
    rt = at("rt", [128, 8, 16], F32, S1 + 47104)
    rW = at("rW", [128, 2, 16], BF16, S1 + 47616)
    rbiasb = at("rbiasb", [128, 16], F32, S1 + 47744)

    xr = at("xr", [128, 2, D], BF16, S2)
    xnt = at("xnt", [128, 2, D], BF16, S2 + 8192)
    xfm = at("xfm", [128, 2, DC, 512], BF16, S2 + 16384)
    xnown = at("xnown", [128, DC, TOWN], BF16, S2)
    wgr = at("wgr", [128, 2, DC, 128], BF16, S2 + 32768)
    wur = at("wur", [128, 2, DC, 128], BF16, S2 + 40960)
    wproj_a = at("wproj_a", [128, 4, D], BF16, S2 + 32768)
    wdx = at("wdx", [128, IC, 512], BF16, S2 + 40960)
    xo5a = at("xo5a", [128, D], F32, S2 + 32768)
    xo5b = at("xo5b", [128, D], F32, S1)

    c3 = S3
    ident_sb = at("ident_sb", [128, 128], BF16, c3); c3 += 256
    ones_k = at("ones_k", [128, 1], BF16, c3); c3 += 64
    bqt_sb = at("bqt_sb", [128, 8], F32, c3); c3 += 64
    bvb = at("bvb", [128, 512], F32, c3); c3 += 2048
    lnt = at("lnt", [128, 64], F32, c3); c3 += 256
    ones_r = at("ones_r", [1, 128], F32, c3); c3 += 512
    wr_sb = at("wr_sb", [128, DC * E], BF16, c3); c3 += 512
    sg_sb = at("sg_sb", [128, 1024], BF16, c3); c3 += 2048
    uw_sb = at("uw_sb", [128, 1024], BF16, c3); c3 += 2048
    shift_sb = at("shift_sb", [128, 1], F32, c3); c3 += 64
    assert c3 <= HI, c3

    pA = nc.alloc_psum_tensor("pA", [128, 2048], F32)
    pB = nc.alloc_psum_tensor("pB", [128, 1024], F32)
    pU = nc.alloc_psum_tensor("pU", [128, 1024], F32)

    tap_list = []

    def tap(name, ap, waits):
        if name not in taps:
            return
        t_out = nc.dram_tensor("tap_" + name, list(ap.shape), ap.dtype,
                               kind="ExternalOutput").ap()
        P.dma("sync", lambda e, o=t_out, a=ap: e.dma_start(out=o, in_=a),
              waits=waits, sem="sQfin")
        tap_list.append(name)

    norm_marks = {}
    v4_marks = {}
    tr_marks = {}
    ev_marks = {}
    predma = {}

    P.dma("sync", lambda e: e.dma_start(out=ident_sb[:, :], in_=ident_dr.ap()), sem="sQc")
    P.dma("sync", lambda e: e.dma_start(out=ones_k[:, :], in_=ones_k_dr.ap()), sem="sQc")
    P.dma("sync", lambda e: e.dma_start(out=ones_r[:, :], in_=ones_r_dr.ap()), sem="sQc")
    P.dma("sync", lambda e: e.dma_start(out=bqt_sb[:, :], in_=bqt_in[:, :]), sem="sQc")
    v_c0 = P.dma("sync", lambda e: e.dma_start(out=bvb[0:1, :], in_=bv_row[:, :]),
                 sem="sQc")
    for i in range(2):
        predma[("a", i)] = P.dma(
            "gpsimd",
            lambda e, a=x_seq[128 * i:128 * i + 128, :], b=xr[:, i % 2, :]:
            e.dma_start(out=b, in_=a), sem="sWx_a%d" % (i % 2))
    for dc in range(DC):
        v_wq = P.dma("gpsimd",
                     lambda e, dc=dc: e.dma_start(
                         out=wqkv_sb[:, dc, :, :],
                         in_=wqkv[128 * dc:128 * dc + 128, :].rearrange(
                             "p (j c) -> p j c", c=128)),
                     sem="sWq")
    v_bvbmm = P.emit("tensor",
                     lambda e: e.matmul(pB[:, 0:512], ones_r[:, :], bvb[0:1, :],
                                        start=True, stop=True),
                     waits=[("sQc", v_c0)], inc="sT")
    v_bvb = P.emit("scalar", lambda e: e.copy(bvb[:, :], pB[:, 0:512]),
                   waits=[("sT", v_bvbmm)], inc="sS")
    v_shift = P.emit("vector", lambda e: e.memset(shift_sb[:, :], SM_SHIFT), inc="sV")

    NSUB = TSEQ // 128


    def emit_ln_tile(i, src_ap, dstbuf, ring_key, xout):
        par = i % 2
        sem = "sWx_%s%d" % (ring_key, par)
        if (ring_key, i) in predma:
            vx = predma[(ring_key, i)]
        else:
            wd = []
            if i >= 2:
                wd = [("sS", norm_marks[(ring_key, i - 2)])]
            vx = P.dma("gpsimd",
                       lambda e, a=src_ap, b=dstbuf[:, par, :]: e.dma_start(out=b, in_=a),
                       waits=wd, sem=sem)
        s = 8 * par
        st1 = lnt[:, s + 0:s + 1]; st2 = lnt[:, s + 1:s + 2]
        m_ = lnt[:, s + 2:s + 3]; var_ = lnt[:, s + 3:s + 4]
        rec_ = lnt[:, s + 4:s + 5]; rstd_ = lnt[:, s + 5:s + 6]
        negms_ = lnt[:, s + 6:s + 7]
        xa = dstbuf[:, par, :]
        prev = norm_marks.get((ring_key, i - 2)) if ring_key == "a" else (
            norm_marks.get(("b", i - 2)) if i >= 2 else norm_marks.get(("a", NSUB - 2 + i)))
        prev_v4 = v4_marks.get((ring_key, i - 2)) if ring_key == "a" else (
            v4_marks.get(("b", i - 2)) if i >= 2 else v4_marks.get(("a", NSUB - 2 + i)))
        w1 = [(sem, vx)]
        if prev:
            w1.append(("sS", prev))
        v1 = P.emit("vector", lambda e, xa=xa, o=st1: e.reduce_sum(o, xa, axis=AX.X),
                    waits=w1, inc="sV")
        trp = tr_marks.get((ring_key, i - 2)) if ring_key == "a" else (
            tr_marks.get(("b", i - 2)) if i >= 2 else tr_marks.get(("a", NSUB - 2 + i)))
        w2 = [(sem, vx)]
        if prev_v4:
            w2.append(("sV", prev_v4))
        if trp:
            w2.append(("sT", trp))
        v2 = P.emit("scalar",
                    lambda e, xa=xa, o=st2, d=xout[:, par, :]: e.activation(
                        d, xa, AF.Square, accum_out=o),
                    waits=w2, inc="sS")
        v3 = P.emit("vector",
                    lambda e, o=m_, a=st1: e.tensor_scalar(o, a, 1.0 / D, None, ALU.mult),
                    waits=[("sV", v1)], inc="sV")
        v4 = P.emit("vector",
                    lambda e, o=var_, a=st2: e.tensor_scalar(o, a, 1.0 / D, None, ALU.mult),
                    waits=[("sS", v2)], inc="sV")
        v5 = P.emit("vector",
                    lambda e, o=var_, a=m_, b=var_: e.scalar_tensor_tensor(
                        o, a, a[:, :], b, ALU.mult, ALU.subtract),
                    waits=[("sV", v4)], inc="sV")
        v6 = P.emit("vector",
                    lambda e, o=var_: e.tensor_scalar(o, var_, -1.0, LN_EPS,
                                                      ALU.mult, ALU.add),
                    waits=[("sV", v5)], inc="sV")
        v7 = P.emit("vector", lambda e, o=rec_, a=var_: e.reciprocal(o, a),
                    waits=[("sV", v6)], inc="sV")
        v8 = P.emit("scalar", lambda e, o=rstd_, a=rec_: e.activation(o, a, AF.Sqrt),
                    waits=[("sV", v7)], inc="sS")
        v9 = P.emit("vector",
                    lambda e, o=negms_, a=m_, b=rstd_: e.tensor_scalar(
                        o, a, b, -1.0, ALU.mult, ALU.mult),
                    waits=[("sS", v8)], inc="sV")
        wn = [("sV", v9)]
        if trp:
            wn.append(("sT", trp))
        vn = P.emit("scalar",
                    lambda e, o=xout[:, par, :], xa=xa, b=negms_, sc=rstd_: e.activation(
                        o, xa, AF.Identity, bias=b, scale=sc),
                    waits=wn, inc="sS")
        norm_marks[(ring_key, i)] = vn
        v4_marks[(ring_key, i)] = v4
        return vn

    tr_half_ev = {0: None, 1: None}

    def emit_transposes(i, ring_key, dst_fn, xsrc, extra_ev_waits=()):
        par = i % 2
        vn = norm_marks[(ring_key, i)]
        ev = None
        for qg in range(4):
            hh = qg % 2
            w = [("sS", vn)]
            if tr_half_ev[hh]:
                w.append(("sV", tr_half_ev[hh]))
            if ring_key == "b" and i == 0 and qg == 0:
                w.append(("sV", mark_p2_yev))
            for j in range(4):
                dc = 4 * qg + j
                mmv = P.emit("tensor",
                             lambda e, o=pU[:, 512 * hh + 128 * j:
                                            512 * hh + 128 * j + 128],
                             a=xsrc[:, par, 128 * dc:128 * dc + 128]: e.matmul(
                                 o, a, ident_sb[:, :], start=True, stop=True),
                             waits=w if j == 0 else [],
                             inc="sT" if j == 3 else None)
            w2 = [("sT", mmv)] + list(extra_ev_waits)
            ev = P.emit("vector",
                        lambda e, o=dst_fn(qg),
                        a=pU[:, 512 * hh:512 * hh + 512].rearrange(
                            "p (j c) -> p j c", c=128): e.tensor_copy(o, a),
                        waits=w2, inc="sV")
            tr_half_ev[hh] = ev
        tr_marks[(ring_key, i)] = mmv
        ev_marks[(ring_key, i)] = ev
        return ev

    qk_ev = {}
    v_ev = {}
    last_group_mm = 0

    def emit_qkv_group(tc, g):
        nonlocal last_group_mm
        cp = tc % 2
        last_ev = ev_marks[("a", 4 * tc + 3)]
        if g < 8:
            qtr = g % 4
            w = [("sV", last_ev), ("sWq", v_wq)]
            pk = qk_ev.get((tc, g - 4)) or qk_ev.get((tc - 1, g + 4))
            if pk:
                w.append(("sS", pk))
            for dc in range(DC):
                mmv = P.emit("tensor",
                             lambda e, o=pA[:, 512 * qtr:512 * qtr + 512],
                             lh=wqkv_sb[:, dc, g, :], rh=xfm[:, cp, dc, :],
                             st=(dc == 0), sp=(dc == DC - 1): e.matmul(
                                 o, lh, rh, start=st, stop=sp),
                             waits=w if dc == 0 else [],
                             inc="sT" if dc == DC - 1 else None)
            last_group_mm = mmv
            dst = q_sb if g < 4 else k_sb
            jj = g % 4
            qk_ev[(tc, g)] = P.emit(
                "scalar",
                lambda e, o=dst[:, jj, 512 * tc:512 * tc + 512],
                a=pA[:, 512 * qtr:512 * qtr + 512], b=bqt_sb[:, g:g + 1]: e.activation(
                    o, a, AF.Identity, bias=b),
                waits=[("sT", mmv)], inc="sS")
        else:
            vts = g - 8
            vh = vts % 2
            gi = 4 * tc + vts
            w = [("sV", last_ev)]
            pv = v_ev.get(gi - 2)
            if pv:
                w.append(("sV", pv))
            if gi == 0:
                w.append(("sS", v_bvb))
            for dc in range(DC):
                mmv = P.emit("tensor",
                             lambda e, o=pB[:, 512 * vh:512 * vh + 512],
                             lh=xfm[:, cp, dc, 128 * vts:128 * vts + 128],
                             rh=wqkv_sb[:, dc, 8:12, :].rearrange("p a b -> p (a b)"),
                             st=(dc == 0), sp=(dc == DC - 1): e.matmul(
                                 o, lh, rh, start=st, stop=sp),
                             waits=w if dc == 0 else [],
                             inc="sT" if dc == DC - 1 else None)
            last_group_mm = mmv
            v_ev[gi] = P.emit(
                "vector",
                lambda e, o=v_sb[:, gi, :], a=pB[:, 512 * vh:512 * vh + 512]:
                e.tensor_tensor(o, a, bvb[:, :], ALU.add),
                waits=[("sT", mmv)], inc="sV")

    def p1_ln(tc, ts):
        i = 4 * tc + ts
        emit_ln_tile(i, x_seq[128 * i:128 * i + 128, :], xr, "a", xnt)

    def p1_tr(tc, ts):
        i = 4 * tc + ts
        ew = [("sT", last_group_mm)] if tc >= 2 else []
        if tc >= 2:
            ew.append(("sQxn", 16 * tc))
        emit_transposes(
            i, "a",
            lambda qg, cp=tc % 2, ts=ts: xfm[:, cp, 4 * qg:4 * qg + 4,
                                             128 * ts:128 * ts + 128],
            xnt, ew)

    def xn_spill(tc):
        P.dma("sync",
              lambda e, o=xn_all.ap()[tc], a=xfm[:, tc % 2, :, :]:
              e.dma_start(out=o, in_=a),
              waits=[("sV", ev_marks[("a", 4 * tc + 3)])], sem="sQxn")

    for tc in range(9):
        if tc < 8:
            p1_ln(tc, 0)
            p1_ln(tc, 1)
        if tc >= 1:
            xn_spill(tc - 1)
        if tc == 0:
            p1_tr(0, 0); p1_ln(0, 2)
            p1_tr(0, 1); p1_ln(0, 3)
            p1_tr(0, 2); p1_tr(0, 3)
            continue
        nts = 0
        for g in range(12):
            emit_qkv_group(tc - 1, g)
            if tc < 8 and g % 3 == 2 and nts < 4:
                p1_tr(tc, nts)
                if nts + 2 <= 3:
                    p1_ln(tc, nts + 2)
                nts += 1

    mark_p1_mm = last_group_mm
    p1_qk_done = max(qk_ev.values())
    p1_v_done = max(v_ev.values())

    tap("q", q_sb[:, :, :], [("sS", p1_qk_done)])
    tap("k", k_sb[:, :, :], [("sS", p1_qk_done)])
    tap("v", v_sb[:, :, :], [("sV", p1_v_done)])

    v_masks = P.dma("sync",
                    lambda e: e.dma_start(
                        out=masks_sb[:, :, :].rearrange("p a b -> p (a b)"),
                        in_=masks_dr.ap()),
                    waits=[("sT", mark_p1_mm)], sem="sQc")

    def xnr_switch(e, p):
        pid = e.partition_id()
        for arm in e.Switch(pid, n=8):
            for half in range(2):
                e.dma_start(out=xnown[:, :, 512 * half:512 * half + 512],
                            in_=xn_all.ap()[2 * (arm % 4) + half]
                            ).then_inc(p.sems["sQxr"], 16)

    v_xnr = P.emit("sync", xnr_switch,
                   waits=[("sT", mark_p1_mm), ("sQxn", 16 * 8),
                          ("sS", norm_marks[("a", NSUB - 1)])],
                   inc="sQxr", add=32, raw=True)
    tap("xnown", xnown[:, :, :], [("sQxr", v_xnr)])

    mask_last = [0]
    exp_half_last = {0: None, 1: None}
    ptile_av_last = {0: None, 1: None}
    yev = {}
    invb_cp = {}
    tail2 = {}
    ny = 0

    pair_list = []
    for h in range(NHL):
        for qt in range(8):
            npair = 2 * (qt + 1)
            for pr in range(npair):
                pair_list.append((h, qt, pr, npair))

    def p2_emit_st_exp(h, qt, pr):
        ph = pr % 2
        w0 = [("sS", p1_qk_done), ("sV", p1_v_done)]
        if exp_half_last[ph]:
            w0.append(("sS", exp_half_last[ph]))
        st_mms = []
        for kk in range(2):
            kc = 2 * pr + kk
            st_mms.append(P.emit(
                "tensor",
                lambda e, o=pA[:, 1024 * ph + 512 * kk:1024 * ph + 512 * kk + 512],
                lh=k_sb[:, h, 128 * kc:128 * kc + 128],
                rh=q_sb[:, h, 512 * qt:512 * qt + 512]: e.matmul(
                    o, lh, rh, start=True, stop=True),
                waits=w0 if kk == 0 else [], inc="sT"))
        mw = []
        for kk in range(2):
            kc = 2 * pr + kk
            o = kc - 4 * qt
            if 0 <= o <= 3:
                mv = P.emit(
                    "vector",
                    lambda e,
                    d=pA[:, 1024 * ph + 512 * kk:1024 * ph + 512 * kk + 512],
                    m=masks_sb[:, o, :]: e.tensor_tensor(d, d, m, ALU.add),
                    waits=[("sT", st_mms[kk]), ("sQc", v_masks)], inc="sV")
                mask_last[0] = mv
                mw.append(("sV", mv))
        slot = pr % 2
        wexp = [("sT", st_mms[1]), ("sV", v_shift)] + mw
        if ptile_av_last[slot]:
            wexp.append(("sT", ptile_av_last[slot]))
        ev = P.emit(
            "scalar",
            lambda e,
            o=ptile[:, 2 * slot:2 * slot + 2, :].rearrange("p a b -> p (a b)"),
            a=pA[:, 1024 * ph:1024 * ph + 1024]: e.activation(
                o, a, AF.Exp, bias=shift_sb[:, :], scale=SM_SCALE),
            waits=wexp, inc="sS")
        exp_half_last[ph] = ev
        return ev

    def p2_emit_av(h, qt, pr, ev):
        u = 8 * h + qt
        pi = u % 2
        yb = pB if pi == 0 else pU
        slot = pr % 2
        for kk in range(2):
            kc = 2 * pr + kk
            wav = []
            if kk == 0:
                wav = [("sS", ev)]
                if kc == 0 and u >= 2:
                    wav += [("sV", yev[u - 2]), ("sS", invb_cp[u - 2])]
            P.emit("tensor",
                   lambda e, o=yb[:, 0:512], lh=v_sb[:, kc, 128 * h:128 * h + 128],
                   rh=ptile[:, 2 * slot + kk, :],
                   st=(kc == 0), sp=(kc == 4 * qt + 3): e.matmul(
                       o, lh, rh, start=st, stop=sp),
                   waits=wav, inc=None)
            mmv = P.emit("tensor",
                         lambda e, o=yb[0:1, 512:1024], lh=ones_k[:, :],
                         rh=ptile[:, 2 * slot + kk, :],
                         st=(kc == 0), sp=(kc == 4 * qt + 3): e.matmul(
                             o, lh, rh, start=st, stop=sp),
                         waits=[], inc="sT")
        ptile_av_last[slot] = mmv
        return mmv

    def p2_tail_a(u, last_av):
        pi = u % 2
        yb = pB if pi == 0 else pU
        t1 = P.emit("vector",
                    lambda e, o=sumtmp[:, pi, :], a=yb[0:1, 512:1024]:
                    e.tensor_scalar(o, a, SUM_EPS, None, ALU.add),
                    waits=[("sT", last_av)], inc="sV")
        t2 = P.emit("vector",
                    lambda e, o=inv_sb[:, pi, :], a=sumtmp[:, pi, :]:
                    e.reciprocal(o, a),
                    waits=[("sV", t1)], inc="sV")
        tail2[u] = t2

    def p2_tail_b(u, h, qt):
        nonlocal ny
        pi = u % 2
        yb = pB if pi == 0 else pU
        t2 = tail2.pop(u)
        t3 = P.emit("tensor",
                    lambda e, o=yb[:, 512:1024], a=inv_sb[:, pi, :]: e.matmul(
                        o, ones_r[:, :], a, start=True, stop=True),
                    waits=[("sV", t2)], inc="sT")
        t4 = P.emit("scalar",
                    lambda e, o=invb_ev[:, pi, :], a=yb[:, 512:1024]: e.copy(o, a),
                    waits=[("sT", t3)], inc="sS")
        invb_cp[u] = t4
        wy = [("sS", t4)]
        if ny >= 2:
            for hh in range(NHL):
                c = min(max(ny - 8 * hh, 0), 8)
                if c > 0:
                    wy.append(("sYh%d" % hh, 16 * c))
        t5 = P.emit("vector",
                    lambda e, o=ynorm[:, pi, :], a=yb[:, 0:512],
                    b=invb_ev[:, pi, :]: e.tensor_tensor(o, a, b, ALU.mult),
                    waits=wy, inc="sV")
        yev[u] = t5
        ny += 1
        P.dma("sync",
              lambda e, o=ag_in.ap()[128 * h:128 * h + 128, 512 * qt:512 * qt + 512],
              a=ynorm[:, pi, :]: e.dma_start(out=o, in_=a),
              waits=[("sV", t5)], sem="sYh%d" % h)

    def emit_ag(l):
        P.emit("gpsimd",
               lambda e, p, l_=l: e.collective_compute(
                   "AllGather", ALU.bypass,
                   replica_groups=[[0, 1, 2, 3], [4, 5, 6, 7]],
                   ins=[ag_in.ap()[128 * l_:128 * l_ + 128, :].opt()],
                   outs=[ag_outh[l_].ap().opt()]
               ).then_inc(p.sems["sCC"], 1),
               waits=[("sYh%d" % l, 16 * 8)], inc="sCC", raw=True)

    pend_av = None
    pend_tb = []
    ag_next = 0
    for (h, qt, pr, npair) in pair_list:
        while ag_next < h:
            emit_ag(ag_next)
            ag_next += 1
        ev = p2_emit_st_exp(h, qt, pr)
        for e_ in pend_tb:
            e_[3] -= 1
        while pend_tb and pend_tb[0][3] <= 0:
            uu, hh, qq, _ = pend_tb.pop(0)
            p2_tail_b(uu, hh, qq)
        if pend_av is not None:
            ph_, qh_, pp_, pe_, last_ = pend_av
            mmv = p2_emit_av(ph_, qh_, pp_, pe_)
            if last_:
                uu = 8 * ph_ + qh_
                p2_tail_a(uu, mmv)
                pend_tb.append([uu, ph_, qh_, 2])
        pend_av = (h, qt, pr, ev, pr == npair - 1)
    ph_, qh_, pp_, pe_, last_ = pend_av
    mmv = p2_emit_av(ph_, qh_, pp_, pe_)
    p2_tail_a(8 * ph_ + qh_, mmv)
    pend_tb.append([8 * ph_ + qh_, ph_, qh_, 0])
    for uu, hh, qq, _ in pend_tb:
        p2_tail_b(uu, hh, qq)

    mark_p2_mm = P.cnt["sT"]
    mark_p2_yev = max(yev.values())
    mark_p2_invb = max(invb_cp.values())
    mark_p2_exp = max(exp_half_last.values())
    mark_p2_mask = mask_last[0]


    v_wr = P.dma("gpsimd",
                 lambda e: e.dma_start(
                     out=wr_sb[:, :].rearrange("p (a e) -> p a e", e=E),
                     in_=wr_t.rearrange("(a p) e -> p a e", p=128)),
                 sem="sWc")
    v_c3 = 0
    for fn in (lambda e: e.dma_start(out=rbiasb[0:1, :], in_=rbias_row[:, :]),
               lambda e: e.dma_start(out=bprojb[0:1, :], in_=bproj_row[:, :]),
               lambda e: e.dma_start(out=onehot_sb[:, :], in_=onehot_dr.ap())):
        v_c3 = P.dma("sync", fn, waits=[("sT", mark_p1_mm)], sem="sQc")
    w_ps = [("sT", mark_p2_mm), ("sQc", v_c3), ("sS", mark_p2_exp),
            ("sV", mark_p2_mask)]
    t1 = P.emit("tensor",
                lambda e: e.matmul(pA[:, 1536:1552], ones_r[:, :], rbiasb[0:1, :],
                                   start=True, stop=True),
                waits=w_ps, inc="sT")
    v_rb = P.emit("scalar", lambda e: e.copy(rbiasb[:, :], pA[:, 1536:1552]),
                  waits=[("sT", t1)], inc="sS")
    bp_ev = []
    for gph in range(4):
        t1 = P.emit("tensor",
                    lambda e, g=gph: e.matmul(pA[:, 512 * g:512 * g + 512], ones_r[:, :],
                                              bprojb[0:1, 512 * g:512 * g + 512],
                                              start=True, stop=True),
                    waits=w_ps + [("sS", v_rb)], inc="sT")
        bp_ev.append(P.emit("scalar",
                            lambda e, g=gph: e.copy(bprojb[:, 512 * g:512 * g + 512],
                                                    pA[:, 512 * g:512 * g + 512]),
                            waits=[("sT", t1)], inc="sS"))
    v_bproj = max(bp_ev)

    rl_ev = {}
    wT_tr = {}

    def emit_router_ts(tsb):
        pq = tsb % 2
        w = [("sQxr", v_xnr), ("sWc", v_wr), ("sS", v_rb), ("sS", v_bproj)]
        if tsb >= 2:
            w.append(("sV", rl_ev[tsb - 2]))
        for dc in range(DC):
            mmv = P.emit("tensor",
                         lambda e, o=pA[:, 512 * pq:512 * pq + 16],
                         lh=xnown[:, dc, 128 * tsb:128 * tsb + 128],
                         rh=wr_sb[:, 16 * dc:16 * dc + 16],
                         st=(dc == 0), sp=(dc == DC - 1): e.matmul(
                             o, lh, rh, start=st, stop=sp),
                         waits=w if dc == 0 else [],
                         inc="sT" if dc == DC - 1 else None)
        L = rt[:, 4 * pq + 0, :]; p_ = rt[:, 4 * pq + 1, :]
        eqa = rt[:, 4 * pq + 2, :]; pm = rt[:, 4 * pq + 3, :]
        s = 8 * pq
        negm = lnt[:, s + 0:s + 1]; m1 = lnt[:, s + 1:s + 2]
        m2 = lnt[:, s + 2:s + 3]; ssum = lnt[:, s + 3:s + 4]
        den = lnt[:, s + 4:s + 5]; w1 = lnt[:, s + 5:s + 6]
        w2 = lnt[:, s + 6:s + 7]
        wl = [("sT", mmv)]
        wl.append(("sS", norm_marks[("a", NSUB - 1)]))
        a1 = P.emit("vector",
                    lambda e, o=L, a=pA[:, 512 * pq:512 * pq + 16]: e.tensor_tensor(
                        o, a, rbiasb[:, :], ALU.add),
                    waits=wl, inc="sV")
        a2m = P.emit("vector", lambda e, o=negm, a=L: e.reduce_max(o, a, axis=AX.X),
                     waits=[("sV", a1)], inc="sV")
        a3 = P.emit("vector",
                    lambda e, o=negm: e.tensor_scalar(o, o, -1.0, None, ALU.mult),
                    waits=[("sV", a2m)], inc="sV")
        a4 = P.emit("scalar",
                    lambda e, o=p_, a=L, b=negm: e.activation(o, a, AF.Exp, bias=b),
                    waits=[("sV", a3)], inc="sS")
        a5 = P.emit("vector", lambda e, o=ssum, a=p_: e.reduce_sum(o, a, axis=AX.X),
                    waits=[("sS", a4)], inc="sV")
        a6 = P.emit("vector", lambda e, o=m1, a=p_: e.reduce_max(o, a, axis=AX.X),
                    waits=[("sS", a4)], inc="sV")
        a7 = P.emit("vector",
                    lambda e, o=eqa, a=p_, b=m1: e.tensor_scalar(o, a, b, None,
                                                                 ALU.is_equal),
                    waits=[("sV", a6)], inc="sV")
        a8 = P.emit("vector",
                    lambda e, o=pm, a=eqa: e.tensor_scalar(o, a, -1.0, 1.0,
                                                           ALU.mult, ALU.add),
                    waits=[("sV", a7)], inc="sV")
        a9 = P.emit("vector",
                    lambda e, o=pm, a=pm, b=p_: e.tensor_tensor(o, a, b, ALU.mult),
                    waits=[("sV", a8)], inc="sV")
        a10 = P.emit("vector", lambda e, o=m2, a=pm: e.reduce_max(o, a, axis=AX.X),
                     waits=[("sV", a9)], inc="sV")
        a11 = P.emit("vector",
                     lambda e, o=pm, a=pm, b=m2: e.tensor_scalar(o, a, b, None,
                                                                 ALU.is_equal),
                     waits=[("sV", a10)], inc="sV")
        a12 = P.emit("vector",
                     lambda e, o=den, a=ssum, b=m1: e.scalar_tensor_tensor(
                         o, a, RT_EPS, b, ALU.mult, ALU.add),
                     waits=[("sV", a10)], inc="sV")
        a13 = P.emit("vector",
                     lambda e, o=den, a=den, b=m2: e.tensor_tensor(o, a, b, ALU.add),
                     waits=[("sV", a12)], inc="sV")
        a14 = P.emit("vector", lambda e, o=den, a=den: e.reciprocal(o, a),
                     waits=[("sV", a13)], inc="sV")
        a15 = P.emit("vector",
                     lambda e, o=w1, a=m1, b=den: e.tensor_tensor(o, a, b, ALU.mult),
                     waits=[("sV", a14)], inc="sV")
        a16 = P.emit("vector",
                     lambda e, o=w2, a=m2, b=den: e.tensor_tensor(o, a, b, ALU.mult),
                     waits=[("sV", a15)], inc="sV")
        a17 = P.emit("vector",
                     lambda e, o=eqa, a=eqa, b=w1: e.tensor_scalar(o, a, b, None,
                                                                   ALU.mult),
                     waits=[("sV", a16)], inc="sV")
        a18 = P.emit("vector",
                     lambda e, o=pm, a=pm, b=w2: e.tensor_scalar(o, a, b, None,
                                                                 ALU.mult),
                     waits=[("sV", a17)], inc="sV")
        wrw = [("sV", a18)]
        if tsb >= 2:
            wrw.append(("sT", wT_tr[tsb - 2]))
        a19 = P.emit("vector",
                     lambda e, o=rW[:, pq, :], a=eqa, b=pm: e.tensor_tensor(
                         o, a, b, ALU.add),
                     waits=wrw, inc="sV")
        rl_ev[tsb] = a19

    def emit_wT(tsb):
        w = [("sV", rl_ev[tsb])]
        if tsb == 0:
            w.append(("sS", v_bproj))
        wT_tr[tsb] = P.emit("tensor",
                            lambda e, o=pA[0:16, 1024 + 128 * tsb:1152 + 128 * tsb],
                            a=rW[:, tsb % 2, :]: e.matmul(o, a, ident_sb[:, :],
                                                          start=True, stop=True),
                            waits=w, inc="sT")

    silu_m = {}; uw_m = {}; a2_m = {}; wbc_m = {}; dpev = {}
    gu_dma = {}; wd_dma = {}
    trk = {"pA0": None, "pA1": None, "pBU_s": None, "pBU_v": None}

    def gu_srcs(e):
        if e < 0:
            return sWg_in, sWu_in
        return Wg_in[e], Wu_in[e]

    def preload_gu(e, ic, waits):
        par = ic % 2
        sg_, su_ = gu_srcs(e)
        vg = P.dma("gpsimd",
                   lambda e_, a=sg_[:, 128 * ic:128 * ic + 128],
                   b=wgr[:, par, :, :]: e_.dma_start(
                       out=b, in_=a.rearrange("(a p) c -> p a c", p=128)),
                   waits=waits, sem="sWg%d" % par)
        vu = P.dma("gpsimd",
                   lambda e_, a=su_[:, 128 * ic:128 * ic + 128],
                   b=wur[:, par, :, :]: e_.dma_start(
                       out=b, in_=a.rearrange("(a p) c -> p a c", p=128)),
                   waits=[], sem="sWu%d" % par)
        gu_dma[(e, ic)] = (vg, vu)

    def preload_wd(e, hg, slot, waits, dest=None, sem=None):
        src = sWd_in if e < 0 else Wd_in[e]
        dst = wdr[:, slot, :, :] if dest is None else dest
        sn = ("sWd%d" % slot) if sem is None else sem
        v = P.dma(
            "gpsimd",
            lambda e_, a=src[:, 512 * hg:512 * hg + 512], b=dst:
            e_.dma_start(out=b, in_=a.rearrange("(a p) c -> p a c", p=128)),
            waits=waits, sem=sn)
        wd_dma[(e, hg)] = ((lambda icx, d=dst: d[:, icx, :]) if dest is not None
                           else (lambda icx, s=slot: wdr[:, s, icx, :]), sn, v)

    def a2_par(e):
        return (e + 1) % 2

    def emit_gu_glue(ei, e, extra0=(), extra0_u=(), interleave=None):
        for ic in range(IC):
            par = ic % 2
            vg, vu = gu_dma[(e, ic)]
            w = [("sWg%d" % par, vg)] + (list(extra0) if ic == 0 else [])
            if ei == 0 and ic == 0:
                w += [("sV", mark_p2_yev), ("sS", mark_p2_invb),
                      ("sQxr", v_xnr)]
            ps1 = silu_m.get((ei, ic - 1)) or (silu_m.get((ei - 1, IC - 1)) if ic == 0 else None)
            if ps1:
                w.append(("sS", ps1))
            for tt in range(2):
                for dc in range(DC):
                    gmm = P.emit("tensor",
                                 lambda e_, o=pB[:, 512 * tt:512 * tt + 512],
                                 lh=wgr[:, par, dc, :],
                                 rh=xnown[:, dc, 512 * tt:512 * tt + 512],
                                 st=(dc == 0), sp=(dc == DC - 1): e_.matmul(
                                     o, lh, rh, start=st, stop=sp),
                                 waits=w if (tt == 0 and dc == 0) else [],
                                 inc="sT" if (tt == 1 and dc == DC - 1) else None)
            w = [("sWu%d" % par, vu)] + (list(extra0_u) if ic == 0 else [])
            pu1 = a2_m.get((ei, ic - 1)) or (a2_m.get((ei - 1, IC - 1)) if ic == 0 else None)
            if pu1:
                w.append(("sV", pu1))
            if ei == 0 and ic == 0:
                w += [("sV", mark_p2_yev), ("sS", mark_p2_invb)]
            for tt in range(2):
                for dc in range(DC):
                    umm = P.emit("tensor",
                                 lambda e_, o=pU[:, 512 * tt:512 * tt + 512],
                                 lh=wur[:, par, dc, :],
                                 rh=xnown[:, dc, 512 * tt:512 * tt + 512],
                                 st=(dc == 0), sp=(dc == DC - 1): e_.matmul(
                                     o, lh, rh, start=st, stop=sp),
                                 waits=w if (tt == 0 and dc == 0) else [],
                                 inc="sT" if (tt == 1 and dc == DC - 1) else None)
            wsl = [("sT", gmm)]
            pa2 = a2_m.get((ei, ic - 1)) or (a2_m.get((ei - 1, IC - 1)) if ic == 0 else None)
            if pa2:
                wsl.append(("sV", pa2))
            sv = P.emit("scalar",
                        lambda e_, a=pB[:, 0:1024]: e_.activation(sg_sb[:, :], a,
                                                                  AF.Silu),
                        waits=wsl, inc="sS")
            silu_m[(ei, ic)] = sv
            if e < 0:
                uv = P.emit("vector",
                            lambda e_, a=pU[:, 0:1024]: e_.tensor_copy(uw_sb[:, :], a),
                            waits=[("sT", umm)], inc="sV")
            else:
                uv = P.emit("vector",
                            lambda e_, a=pU[:, 0:1024], b=wbc_sb[:, e % 2, :]:
                            e_.tensor_tensor(uw_sb[:, :], a, b, ALU.mult),
                            waits=[("sT", umm), ("sS", wbc_m[e])], inc="sV")
            uw_m[(ei, ic)] = uv
            av = P.emit("vector",
                        lambda e_, o=a2[:, a2_par(e), ic, :], a=sg_sb[:, :],
                        b=uw_sb[:, :]: e_.tensor_tensor(o, a, b, ALU.mult),
                        waits=[("sS", sv), ("sV", uv)], inc="sV")
            a2_m[(ei, ic)] = av
            if ic + 2 < IC:
                preload_gu(e, ic + 2, [("sT", umm)])
            if interleave is not None:
                interleave(ic)
        return umm

    def emit_wbc(e, extra=()):
        evs = []
        for half in range(2):
            w = list(extra) + [("sS", v_wT)]
            if trk["pA1"]:
                w.append(trk["pA1"])
            t = P.emit("tensor",
                       lambda e_, o=pA[:, 1024 + 512 * half:1024 + 512 * half + 512],
                       lh=onehot_sb[0:16, 128 * e:128 * e + 128],
                       rh=wT_sb[0:16, 512 * half:512 * half + 512]: e_.matmul(
                           o, lh, rh, start=True, stop=True),
                       waits=w if half == 0 else [], inc="sT")
            evs.append(P.emit("scalar",
                              lambda e_, o=wbc_sb[:, e % 2, 512 * half:512 * half + 512],
                              a=pA[:, 1024 + 512 * half:1024 + 512 * half + 512]:
                              e_.copy(o, a),
                              waits=[("sT", t)], inc="sS"))
        wbc_m[e] = max(evs)
        trk["pA1"] = ("sS", wbc_m[e])

    def emit_dp(ei, e, extra_first=()):
        first = True
        for hg in range(4):
            apf, sn, vwd = wd_dma[(e, hg)]
            for ts in range(8):
                ph = ts % 2
                w = [(sn, vwd), ("sV", a2_m[(ei, IC - 1)])]
                if first:
                    w += list(extra_first)
                    first = False
                key = ("dp_half", ph)
                if trk.get(key):
                    w.append(("sV", trk[key]))
                for icx in range(IC):
                    mmv = P.emit("tensor",
                                 lambda e_, o=pA[:, 512 * ph:512 * ph + 512],
                                 lh=a2[:, a2_par(e), icx, 128 * ts:128 * ts + 128],
                                 rh=apf(icx),
                                 st=(icx == 0), sp=(icx == IC - 1): e_.matmul(
                                     o, lh, rh, start=st, stop=sp),
                                 waits=w if icx == 0 else [],
                                 inc="sT" if icx == IC - 1 else None)
                if e < 0:
                    evv = P.emit("vector",
                                 lambda e_, o=acc[:, ts, 512 * hg:512 * hg + 512],
                                 a=pA[:, 512 * ph:512 * ph + 512],
                                 b=bprojb[:, 512 * hg:512 * hg + 512]: e_.tensor_tensor(
                                     o, a, b, ALU.add),
                                 waits=[("sT", mmv), ("sS", v_bproj)], inc="sV")
                else:
                    evv = P.emit("vector",
                                 lambda e_, o=acc[:, ts, 512 * hg:512 * hg + 512],
                                 a=pA[:, 512 * ph:512 * ph + 512]: e_.tensor_tensor(
                                     o, o, a, ALU.add),
                                 waits=[("sT", mmv)], inc="sV")
                dpev[(ei, hg, ts)] = evv
                trk[("dp_half", ph)] = evv
            nxt = dp_wd_chain.pop(0) if dp_wd_chain else None
            if nxt is not None:
                e2, hg2 = nxt
                preload_wd(e2, hg2, int(sn[-1]), [("sT", mmv)])
        return mmv

    preload_gu(-1, 0, [("sT", mark_p1_mm), ("sQxn", 16 * 8)])
    preload_gu(-1, 1, [])
    emit_ag(3)

    def sh_ileave(ic):
        if ic >= 2:
            emit_wT(ic - 2)
        emit_router_ts(ic)

    sh_gu_last = emit_gu_glue(0, -1, interleave=sh_ileave)
    emit_wT(6)
    emit_wT(7)
    v_wT = P.emit("scalar", lambda e: e.copy(wT_sb[:, :], pA[0:16, 1024:2048]),
                  waits=[("sT", wT_tr[7])], inc="sS")
    tap("wT", wT_sb[:, :], [("sS", v_wT)])
    emit_wbc(0, [("sT", mark_p2_mm), ("sV", mark_p2_yev), ("sS", mark_p2_invb)])

    preload_wd(-1, 0, 0, [("sV", mark_p2_yev), ("sS", mark_p2_invb),
                          ("sT", mark_p2_mm)])
    preload_wd(-1, 1, 1, [])
    preload_wd(-1, 2, 2, [])
    dp_wd_chain = [(-1, 3)]
    sh_dp_last = emit_dp(0, -1)
    sh_dp_ev = dpev[(0, 3, 7)]

    def rb_src(h):
        r, l = h // 4, h % 4
        return ag_outh[l].ap()[128 * r:128 * r + 128, :]

    def rb_switch_hi(e, p):
        pid = e.partition_id()
        for arm in e.Switch(pid, n=8):
            off = 1024 * (arm % 4)
            for dc in range(8, DC):
                e.dma_start(out=yown[:, dc, :],
                            in_=rb_src(dc)[:, off:off + 1024]
                            ).then_inc(p.sems["sQyh"], 16)

    def rb_switch_lo(e, p):
        pid = e.partition_id()
        for arm in e.Switch(pid, n=8):
            off = 1024 * (arm % 4)
            for dc in range(8):
                e.dma_start(out=yown[:, dc, :],
                            in_=rb_src(dc)[:, off:off + 1024]
                            ).then_inc(p.sems["sQyo"], 16)

    v_yh = P.emit("sync", rb_switch_hi,
                  waits=[("sCC", 4), ("sT", mark_p2_mm)], inc="sQyh",
                  add=16 * 8, raw=True)
    v_yo = P.emit("sync", rb_switch_lo,
                  waits=[("sCC", 4), ("sT", sh_dp_last)], inc="sQyo",
                  add=16 * 8, raw=True)

    wp_mark = {}

    def load_wproj(qd, waits):
        ring, sname = (wproj_a, "sWpa") if qd % 2 == 0 else (wproj_b, "sWpb")
        w = list(waits)
        for j in range(4):
            dc = 4 * qd + j
            v = P.dma("gpsimd",
                      lambda e, a=w_proj[128 * dc:128 * dc + 128, :], b=ring[:, j, :]:
                      e.dma_start(out=b, in_=a),
                      waits=w if j == 0 else [], sem=sname)
        wp_mark[qd] = v

    load_wproj(2, [("sT", sh_gu_last)])
    load_wproj(3, [("sT", sh_dp_last)])

    prj_last_ev = None
    for qd in (2, 3, 0, 1):
        ring, sname = (wproj_a, "sWpa") if qd % 2 == 0 else (wproj_b, "sWpb")
        for ts in range(8):
            grp = ts % 2
            w = [(sname, wp_mark[qd]),
                 ("sQyh", v_yh) if qd >= 2 else ("sQyo", v_yo)]
            key = ("pj", grp)
            if trk.get(key):
                w.append(("sV", trk[key]))
            elif grp == 0:
                w += [("sS", silu_m[(0, IC - 1)]), ("sV", sh_dp_ev)]
            else:
                w += [("sV", sh_dp_ev), ("sS", wbc_m[0])]
            mms = []
            for gph in range(4):
                if grp == 0:
                    ot = pB[:, 512 * (gph % 2):512 * (gph % 2) + 512] if gph < 2 else \
                        pU[:, 512 * (gph % 2):512 * (gph % 2) + 512]
                else:
                    ot = pA[:, 512 * gph:512 * gph + 512]
                for j in range(4):
                    dc = 4 * qd + j
                    mmv = P.emit("tensor",
                                 lambda e_, o=ot,
                                 lh=yown[:, dc, 128 * ts:128 * ts + 128],
                                 rh=ring[:, j, 512 * gph:512 * gph + 512],
                                 st=(j == 0), sp=(j == 3): e_.matmul(
                                     o, lh, rh, start=st, stop=sp),
                                 waits=w if (gph == 0 and j == 0) else [],
                                 inc="sT" if j == 3 else None)
                mms.append(mmv)
            if grp == 0:
                e1 = P.emit("vector",
                            lambda e_, o=acc[:, ts, 0:1024], a=pB[:, 0:1024]:
                            e_.tensor_tensor(o, o, a, ALU.add),
                            waits=[("sT", mms[1])], inc="sV")
                e2 = P.emit("vector",
                            lambda e_, o=acc[:, ts, 1024:2048], a=pU[:, 0:1024]:
                            e_.tensor_tensor(o, o, a, ALU.add),
                            waits=[("sT", mms[3])], inc="sV")
            else:
                e1 = P.emit("vector",
                            lambda e_, o=acc[:, ts, 0:1024], a=pA[:, 0:1024]:
                            e_.tensor_tensor(o, o, a, ALU.add),
                            waits=[("sT", mms[3])], inc="sV")
                e2 = P.emit("vector",
                            lambda e_, o=acc[:, ts, 1024:2048], a=pA[:, 1024:2048]:
                            e_.tensor_tensor(o, o, a, ALU.add),
                            waits=[("sT", mms[3])], inc="sV")
            trk[("pj", grp)] = e2
        if qd == 2:
            load_wproj(0, [("sT", mmv)])
        elif qd == 3:
            load_wproj(1, [("sT", mmv)])
    proj_last_mm = P.cnt["sT"]
    proj_last_ev = max(trk[("pj", 0)], trk[("pj", 1)])
    trk[("dp_half", 0)] = trk[("pj", 1)]
    trk[("dp_half", 1)] = trk[("pj", 1)]
    trk["pA1"] = ("sV", trk[("pj", 1)])

    preload_gu(0, 0, [("sT", proj_last_mm)])
    preload_gu(0, 1, [])
    preload_wd(0, 0, 0, [("sT", proj_last_mm)])
    preload_wd(0, 1, 1, [])
    preload_wd(0, 2, 2, [])
    dp_wd_chain = []
    for e in range(16):
        for hg in range(4):
            if e == 0 and hg < 3:
                continue
            if e == 15 and hg == 3:
                continue
            dp_wd_chain.append((e, hg))

    for e in range(16):
        ei = e + 1
        if e >= 1:
            exw = [("sV", uw_m[(e - 1, IC - 1)])] if e >= 2 else []
            emit_wbc(e, exw)
        ex0 = []
        if e == 0:
            ex0 = [("sT", proj_last_mm), ("sV", trk[("pj", 0)]),
                   ("sV", trk[("pj", 1)])]
        gu_last = emit_gu_glue(ei, e, ex0)
        if e == 15:
            last_gu_mark = gu_last
            preload_wd(15, 3, 0, [("sT", last_gu_mark)], dest=wdx[:, :, :],
                       sem="sWdx")
        if e + 1 < 16:
            preload_gu(e + 1, 0, [("sT", gu_last)])
            preload_gu(e + 1, 1, [])
        if e >= 1:
            emit_dp(ei - 1, e - 1)

    add_m = {}

    def p5_ts(ts):
        par = ts % 2
        xo = xo5a if par == 0 else xo5b
        if ts < 2:
            w = [("sT", last_gu_mark)]
        else:
            w = [("sV", add_m[ts - 2])]
        vx = P.dma("sync",
                   lambda e, a=x_own[128 * ts:128 * ts + 128, :], b=xo[:, :]:
                   e.dma_start(out=b, in_=a),
                   waits=w, sem="sQxo%d" % par)
        wadd = [("sQxo%d" % par, vx)]
        for hg in range(4):
            wadd.append(("sV", dpev[(16, hg, ts)]))
        va = P.emit("vector",
                    lambda e, o=acc[:, ts, :], b=xo[:, :]: e.tensor_tensor(
                        o, o, b, ALU.add),
                    waits=wadd, inc="sV")
        add_m[ts] = va
        P.dma("sync",
              lambda e, o=out_own[128 * ts:128 * ts + 128, :], a=acc[:, ts, :]:
              e.dma_start(out=o, in_=a),
              waits=[("sV", va)], sem="sQst")

    for ts in range(8):
        for hg in range(4):
            apf, sn, vwd = wd_dma[(15, hg)]
            ph = (4 * ts + hg) % 2
            w = [(sn, vwd), ("sV", a2_m[(16, IC - 1)])]
            key = ("dp_half", ph)
            if trk.get(key):
                w.append(("sV", trk[key]))
            for icx in range(IC):
                mmv = P.emit("tensor",
                             lambda e_, o=pA[:, 512 * ph:512 * ph + 512],
                             lh=a2[:, a2_par(15), icx, 128 * ts:128 * ts + 128],
                             rh=apf(icx),
                             st=(icx == 0), sp=(icx == IC - 1): e_.matmul(
                                 o, lh, rh, start=st, stop=sp),
                             waits=w if icx == 0 else [],
                             inc="sT" if icx == IC - 1 else None)
            evv = P.emit("vector",
                         lambda e_, o=acc[:, ts, 512 * hg:512 * hg + 512],
                         a=pA[:, 512 * ph:512 * ph + 512]: e_.tensor_tensor(
                             o, o, a, ALU.add),
                         waits=[("sT", mmv)], inc="sV")
            dpev[(16, hg, ts)] = evv
            trk[("dp_half", ph)] = evv
        p5_ts(ts)
    all_mm = P.cnt["sT"]

    fin = P.cnt["sQst"]
    fw = [("sQst", fin)]
    if P.cnt.get("sQfin"):
        fw.append(("sQfin", P.cnt["sQfin"]))
    P.emit("sync", lambda e: e.nop(), waits=fw)

    nc._dbg = {"xnown": xnown, "wr_sb": wr_sb, "rbiasb": rbiasb, "rt": rt,
               "xnt_b": xnt_b, "lnt": lnt, "q_sb": q_sb, "k_sb": k_sb,
               "v_sb": v_sb, "acc": acc, "xo3": xo3, "wT_sb": wT_sb,
               "a2": a2, "yown": yown, "wbc_sb": wbc_sb, "xfm": xfm}

    with nc.Block() as block:
        @block.sync
        def _(eng):
            P.replay(eng, "sync")

        @block.gpsimd
        def _(eng):
            P.replay(eng, "gpsimd")

        @block.tensor
        def _(eng):
            P.replay(eng, "tensor")

        @block.scalar
        def _(eng):
            P.replay(eng, "scalar")

        @block.vector
        def _(eng):
            P.replay(eng, "vector")

    return nc


def kernel(L0_attn_x, L0_attn_w_attn, L0_attn_b_attn, L0_attn_w_proj,
           L0_attn_b_proj, L0_ffn_Wr, L0_ffn_bias, L0_ffn_Wg, L0_ffn_Wu,
           L0_ffn_Wd, L0_ffn_sWg, L0_ffn_sWu, L0_ffn_sWd):
    x = np.ascontiguousarray(np.asarray(L0_attn_x, dtype=np.float32))
    w_attn = np.asarray(L0_attn_w_attn, dtype=np.float32)
    b_attn = np.asarray(L0_attn_b_attn, dtype=np.float32)
    w_projv = np.ascontiguousarray(np.asarray(L0_attn_w_proj, dtype=np.float32))
    b_proj = np.asarray(L0_attn_b_proj, dtype=np.float32)
    Wr = np.asarray(L0_ffn_Wr, dtype=np.float32)
    rbias = np.asarray(L0_ffn_bias, dtype=np.float32)
    Wg = np.ascontiguousarray(np.asarray(L0_ffn_Wg, dtype=np.float32))
    Wu = np.ascontiguousarray(np.asarray(L0_ffn_Wu, dtype=np.float32))
    Wd = np.ascontiguousarray(np.asarray(L0_ffn_Wd, dtype=np.float32))
    sWg = np.ascontiguousarray(np.asarray(L0_ffn_sWg, dtype=np.float32))
    sWu = np.ascontiguousarray(np.asarray(L0_ffn_sWu, dtype=np.float32))
    sWd = np.ascontiguousarray(np.asarray(L0_ffn_sWd, dtype=np.float32))

    nc = build_program()

    in_maps = []
    for c in range(8):
        s, q4 = c // 4, c % 4
        wq = w_attn[:, 512 * q4:512 * (q4 + 1)]
        wk = w_attn[:, 2048 + 512 * q4:2048 + 512 * (q4 + 1)]
        wv = w_attn[:, 4096 + 512 * q4:4096 + 512 * (q4 + 1)]
        bq = b_attn[512 * q4:512 * (q4 + 1)]
        bk = b_attn[2048 + 512 * q4:2048 + 512 * (q4 + 1)]
        bv = b_attn[4096 + 512 * q4:4096 + 512 * (q4 + 1)]
        in_maps.append({
            "x_seq": np.ascontiguousarray(x[s]),
            "x_own": np.ascontiguousarray(x[s, 1024 * q4:1024 * (q4 + 1)]),
            "wqkv": np.ascontiguousarray(np.concatenate([wq, wk, wv], axis=1)),
            "bqt_in": np.ascontiguousarray(
                np.concatenate([bq.reshape(4, 128).T, bk.reshape(4, 128).T], axis=1)),
            "bv_row": np.ascontiguousarray(bv[None, :]),
            "w_proj": w_projv,
            "bproj_row": np.ascontiguousarray(b_proj[None, :]),
            "wr_t": np.ascontiguousarray(Wr.T),
            "rbias_row": np.ascontiguousarray(rbias[None, :]),
            "Wg_in": Wg, "Wu_in": Wu, "Wd_in": Wd,
            "sWg_in": sWg, "sWu_in": sWu, "sWd_in": sWd,
        })

    res = run_bass_kernel_spmd(nc, in_maps, core_ids=list(range(8)))
    out = np.empty((2, 4096, 2048), dtype=np.float32)
    for c in range(8):
        s, q4 = c // 4, c % 4
        out[s, 1024 * q4:1024 * (q4 + 1)] = res.results[c]["out_own"]
    return out


# version 09 (final)
# speedup vs baseline: 1.0341x; 1.0341x over previous
import numpy as np
import ml_dtypes

import concourse.bass as bass
import concourse.mybir as mybir
from concourse.bass_utils import run_bass_kernel_spmd

F32 = mybir.dt.float32
BF16 = mybir.dt.bfloat16
AF = mybir.ActivationFunctionType
ALU = mybir.AluOpType
AX = mybir.AxisListType
BF = ml_dtypes.bfloat16

D = 2048
DC = 16
TSEQ = 4096
TOWN = 1024
NHL = 4
E = 16
DI = 1024
IC = 8
SM_SCALE = 1.0 / float(np.sqrt(128.0))
SM_SHIFT = -24.0
NEG_BIG = -1.0e9
LN_EPS = 1e-5
RT_EPS = 1e-9
SUM_EPS = 1e-30

ENGS = ("sync", "gpsimd", "tensor", "scalar", "vector")


class Plan:
    def __init__(self, nc):
        self.nc = nc
        self.ops = {e: [] for e in ENGS}
        self.cnt = {}
        self.sems = {}

    def sem(self, name):
        if name not in self.sems:
            self.sems[name] = self.nc.alloc_semaphore(name)
            self.cnt[name] = 0
        return self.sems[name]

    def emit(self, eng, fn, waits=(), inc=None, add=1, raw=False):
        val = None
        if inc is not None:
            self.sem(inc)
            self.cnt[inc] += add
            val = self.cnt[inc]
        w = tuple((s, v) for (s, v) in waits if v is not None and v > 0)
        for s, _ in w:
            self.sem(s)
        self.ops[eng].append((fn, w, inc, add, raw))
        return val

    def dma(self, qeng, fn, waits=(), sem=None):
        return self.emit(qeng, fn, waits, inc=sem, add=16)

    def replay(self, eng_obj, name):
        hw = {}
        for fn, waits, inc, add, raw in self.ops[name]:
            for s, v in waits:
                if hw.get(s, 0) < v:
                    eng_obj.wait_ge(self.sems[s], v)
                    hw[s] = v
            if raw:
                fn(eng_obj, self)
            else:
                ins = fn(eng_obj)
                if inc is not None:
                    ins.then_inc(self.sems[inc], add)


def build_program(taps=None):
    taps = set(taps or ())
    nc = bass.Bass("TRN2", target_bir_lowering=False, debug=False)
    P = Plan(nc)

    def dri(n, s, dt=F32):
        return nc.dram_tensor(n, s, dt, kind="ExternalInput").ap()

    x_seq = dri("x_seq", [TSEQ, D])
    x_own = dri("x_own", [TOWN, D])
    wqkv = dri("wqkv", [D, 1536])
    bqt_in = dri("bqt_in", [128, 8])
    bv_row = dri("bv_row", [1, 512])
    w_proj = dri("w_proj", [D, D])
    bproj_row = dri("bproj_row", [1, D])
    wr_t = dri("wr_t", [D, E])
    rbias_row = dri("rbias_row", [1, E])
    Wg_in = dri("Wg_in", [E, D, DI])
    Wu_in = dri("Wu_in", [E, D, DI])
    Wd_in = dri("Wd_in", [E, DI, D])
    sWg_in = dri("sWg_in", [D, DI])
    sWu_in = dri("sWu_in", [D, DI])
    sWd_in = dri("sWd_in", [DI, D])

    out_own = nc.dram_tensor("out_own", [TOWN, D], F32, kind="ExternalOutput").ap()

    ag_in = nc.dram_tensor("ag_in", [512, TSEQ], BF16, kind="Internal")
    ag_outh = [nc.dram_tensor("ag_outh%d" % l, [512, TSEQ], BF16, kind="Internal")
               for l in range(NHL)]
    xn_all = nc.dram_tensor("xn_all", [8, 128, DC, 512], BF16, kind="Internal")

    ident_dr = nc.inline_tensor(np.eye(128, dtype=BF), name="c_ident")
    mk = np.zeros((128, 4, 512), dtype=np.float32)
    for o in range(4):
        kkk = np.arange(128)[:, None] + 128 * o
        qqq = np.arange(512)[None, :]
        mk[:, o, :] = np.where(kkk > qqq, NEG_BIG, 0.0)
    masks_dr = nc.inline_tensor(mk.astype(BF).reshape(128, 4 * 512), name="c_masks")
    ones_k_dr = nc.inline_tensor(np.ones((128, 1), dtype=BF), name="c_onesk")
    ones_kk_dr = nc.inline_tensor(np.ones((128, 128), dtype=BF), name="c_oneskk")
    ones_r_dr = nc.inline_tensor(np.ones((1, 128), dtype=np.float32), name="c_onesr")
    oh = np.zeros((16, 16, 128), dtype=np.float32)
    for ee in range(16):
        oh[ee, ee, :] = 1.0
    onehot_dr = nc.inline_tensor(oh.astype(BF).reshape(16, 16 * 128), name="c_onehot")

    LO, HI = 18432, 229344
    S0 = LO
    S1 = LO + 98304
    S2 = S1 + 49152
    S3 = S2 + 49152

    def at(name, shape, dtype, off):
        esz = 2 if dtype == BF16 else 4
        n = 1
        for s in shape[1:]:
            n *= s
        assert off + n * esz <= HI, (name, off, n * esz)
        return nc.alloc_sbuf_tensor_at(name, shape, dtype, offset=off)

    q_sb = at("q_sb", [128, NHL, TSEQ], BF16, S0)
    k_sb = at("k_sb", [128, NHL, TSEQ], BF16, S0 + 32768)
    v_sb = at("v_sb", [128, 32, 512], BF16, S0 + 65536)
    acc = at("acc", [128, 8, D], F32, S0)
    a2 = at("a2", [128, 2, IC, TOWN], BF16, S0 + 65536)
    yown = at("yown", [128, DC, TOWN], BF16, S0 + 65536)

    wqkv_sb = at("wqkv_sb", [128, DC, 12, 128], BF16, S1)
    masks_sb = at("masks_sb", [128, 4, 512], BF16, S1)
    bprojb = at("bprojb", [128, D], F32, S1 + 4096)
    ptile = at("ptile", [128, 4, 512], BF16, S1 + 12288)
    ynorm = at("ynorm", [128, 2, 512], BF16, S1 + 16384)
    inv_sb = at("inv_sb", [128, 2, 512], F32, S1 + 18432)
    sumtmp = at("sumtmp", [128, 2, 512], F32, S1 + 22528)
    xo3 = at("xo3", [128, 2, D], BF16, S1 + 12288)
    xnt_b = at("xnt_b", [128, 2, D], BF16, S1 + 20480)
    wdr = at("wdr", [128, 3, IC, 512], BF16, S1 + 12288)
    wproj_b = at("wproj_b", [128, 4, D], BF16, S1 + 12288)
    wT_sb = at("wT_sb", [16, 1024], BF16, S1 + 36864)
    onehot_sb = at("onehot_sb", [16, 16 * 128], BF16, S1 + 38912)
    wbc_sb = at("wbc_sb", [128, 2, 1024], BF16, S1 + 43008)
    rt = at("rt", [128, 8, 16], F32, S1 + 47104)
    rW = at("rW", [128, 2, 16], BF16, S1 + 47616)
    rbiasb = at("rbiasb", [128, 16], F32, S1 + 47744)

    xr = at("xr", [128, 2, D], BF16, S2)
    xnt = at("xnt", [128, 2, D], BF16, S2 + 8192)
    xfm = at("xfm", [128, 2, DC, 512], BF16, S2 + 16384)
    xnown = at("xnown", [128, DC, TOWN], BF16, S2)
    wgr = at("wgr", [128, 2, DC, 128], BF16, S2 + 32768)
    wur = at("wur", [128, 2, DC, 128], BF16, S2 + 40960)
    wproj_a = at("wproj_a", [128, 4, D], BF16, S2 + 32768)
    wdx = at("wdx", [128, IC, 512], BF16, S2 + 40960)
    xo5a = at("xo5a", [128, D], F32, S2 + 32768)
    xo5b = at("xo5b", [128, D], F32, S1)

    c3 = S3
    ident_sb = at("ident_sb", [128, 128], BF16, c3); c3 += 256
    ones_k = at("ones_k", [128, 1], BF16, c3); c3 += 64
    ones_kk = at("ones_kk", [128, 128], BF16, c3); c3 += 256
    bqt_sb = at("bqt_sb", [128, 8], F32, c3); c3 += 64
    bvb = at("bvb", [128, 512], F32, c3); c3 += 2048
    lnt = at("lnt", [128, 64], F32, c3); c3 += 256
    ones_r = at("ones_r", [1, 128], F32, c3); c3 += 512
    wr_sb = at("wr_sb", [128, DC * E], BF16, c3); c3 += 512
    sg_sb = at("sg_sb", [128, 1024], BF16, c3); c3 += 2048
    uw_sb = at("uw_sb", [128, 1024], BF16, c3); c3 += 2048
    shift_sb = at("shift_sb", [128, 1], F32, c3); c3 += 64
    assert c3 <= HI, c3

    pA = nc.alloc_psum_tensor("pA", [128, 2048], F32)
    pB = nc.alloc_psum_tensor("pB", [128, 1024], F32)
    pU = nc.alloc_psum_tensor("pU", [128, 1024], F32)

    tap_list = []

    def tap(name, ap, waits):
        if name not in taps:
            return
        t_out = nc.dram_tensor("tap_" + name, list(ap.shape), ap.dtype,
                               kind="ExternalOutput").ap()
        P.dma("sync", lambda e, o=t_out, a=ap: e.dma_start(out=o, in_=a),
              waits=waits, sem="sQfin")
        tap_list.append(name)

    norm_marks = {}
    v4_marks = {}
    tr_marks = {}
    ev_marks = {}
    predma = {}

    P.dma("sync", lambda e: e.dma_start(out=ident_sb[:, :], in_=ident_dr.ap()), sem="sQc")
    P.dma("sync", lambda e: e.dma_start(out=ones_k[:, :], in_=ones_k_dr.ap()), sem="sQc")
    P.dma("sync", lambda e: e.dma_start(out=ones_kk[:, :], in_=ones_kk_dr.ap()), sem="sQc")
    P.dma("sync", lambda e: e.dma_start(out=ones_r[:, :], in_=ones_r_dr.ap()), sem="sQc")
    P.dma("sync", lambda e: e.dma_start(out=bqt_sb[:, :], in_=bqt_in[:, :]), sem="sQc")
    v_c0 = P.dma("sync", lambda e: e.dma_start(out=bvb[0:1, :], in_=bv_row[:, :]),
                 sem="sQc")
    for i in range(2):
        predma[("a", i)] = P.dma(
            "gpsimd",
            lambda e, a=x_seq[128 * i:128 * i + 128, :], b=xr[:, i % 2, :]:
            e.dma_start(out=b, in_=a), sem="sWx_a%d" % (i % 2))
    v_wq3 = {}
    for third in range(3):
        for dc in range(DC):
            v_wq3[third] = P.dma(
                "gpsimd",
                lambda e, dc=dc, t=third: e.dma_start(
                    out=wqkv_sb[:, dc, 4 * t:4 * t + 4, :],
                    in_=wqkv[128 * dc:128 * dc + 128, 512 * t:512 * t + 512]
                    .rearrange("p (j c) -> p j c", c=128)),
                sem="sWq%d" % third)
    v_bvbmm = P.emit("tensor",
                     lambda e: e.matmul(pB[:, 0:512], ones_r[:, :], bvb[0:1, :],
                                        start=True, stop=True),
                     waits=[("sQc", v_c0)], inc="sT")
    v_bvb = P.emit("scalar", lambda e: e.copy(bvb[:, :], pB[:, 0:512]),
                   waits=[("sT", v_bvbmm)], inc="sS")
    v_shift = P.emit("vector", lambda e: e.memset(shift_sb[:, :], SM_SHIFT), inc="sV")

    NSUB = TSEQ // 128


    def emit_ln_tile(i, src_ap, dstbuf, ring_key, xout):
        par = i % 2
        sem = "sWx_%s%d" % (ring_key, par)
        if (ring_key, i) in predma:
            vx = predma[(ring_key, i)]
        else:
            wd = []
            if i >= 2:
                wd = [("sS", norm_marks[(ring_key, i - 2)])]
            vx = P.dma("gpsimd",
                       lambda e, a=src_ap, b=dstbuf[:, par, :]: e.dma_start(out=b, in_=a),
                       waits=wd, sem=sem)
        s = 8 * par
        st1 = lnt[:, s + 0:s + 1]; st2 = lnt[:, s + 1:s + 2]
        m_ = lnt[:, s + 2:s + 3]; var_ = lnt[:, s + 3:s + 4]
        rec_ = lnt[:, s + 4:s + 5]; rstd_ = lnt[:, s + 5:s + 6]
        negms_ = lnt[:, s + 6:s + 7]
        xa = dstbuf[:, par, :]
        prev = norm_marks.get((ring_key, i - 2)) if ring_key == "a" else (
            norm_marks.get(("b", i - 2)) if i >= 2 else norm_marks.get(("a", NSUB - 2 + i)))
        prev_v4 = v4_marks.get((ring_key, i - 2)) if ring_key == "a" else (
            v4_marks.get(("b", i - 2)) if i >= 2 else v4_marks.get(("a", NSUB - 2 + i)))
        w1 = [(sem, vx)]
        if prev:
            w1.append(("sS", prev))
        v1 = P.emit("vector", lambda e, xa=xa, o=st1: e.reduce_sum(o, xa, axis=AX.X),
                    waits=w1, inc="sV")
        trp = tr_marks.get((ring_key, i - 2)) if ring_key == "a" else (
            tr_marks.get(("b", i - 2)) if i >= 2 else tr_marks.get(("a", NSUB - 2 + i)))
        w2 = [(sem, vx)]
        if prev_v4:
            w2.append(("sV", prev_v4))
        if trp:
            w2.append(("sT", trp))
        v2 = P.emit("scalar",
                    lambda e, xa=xa, o=st2, d=xout[:, par, :]: e.activation(
                        d, xa, AF.Square, accum_out=o),
                    waits=w2, inc="sS")
        v3 = P.emit("vector",
                    lambda e, o=m_, a=st1: e.tensor_scalar(o, a, 1.0 / D, None, ALU.mult),
                    waits=[("sV", v1)], inc="sV")
        v4 = P.emit("vector",
                    lambda e, o=var_, a=st2: e.tensor_scalar(o, a, 1.0 / D, None, ALU.mult),
                    waits=[("sS", v2)], inc="sV")
        v5 = P.emit("vector",
                    lambda e, o=var_, a=m_, b=var_: e.scalar_tensor_tensor(
                        o, a, a[:, :], b, ALU.mult, ALU.subtract),
                    waits=[("sV", v4)], inc="sV")
        v6 = P.emit("vector",
                    lambda e, o=var_: e.tensor_scalar(o, var_, -1.0, LN_EPS,
                                                      ALU.mult, ALU.add),
                    waits=[("sV", v5)], inc="sV")
        v7 = P.emit("vector", lambda e, o=rec_, a=var_: e.reciprocal(o, a),
                    waits=[("sV", v6)], inc="sV")
        v8 = P.emit("scalar", lambda e, o=rstd_, a=rec_: e.activation(o, a, AF.Sqrt),
                    waits=[("sV", v7)], inc="sS")
        v9 = P.emit("vector",
                    lambda e, o=negms_, a=m_, b=rstd_: e.tensor_scalar(
                        o, a, b, -1.0, ALU.mult, ALU.mult),
                    waits=[("sS", v8)], inc="sV")
        wn = [("sV", v9)]
        if trp:
            wn.append(("sT", trp))
        vn = P.emit("scalar",
                    lambda e, o=xout[:, par, :], xa=xa, b=negms_, sc=rstd_: e.activation(
                        o, xa, AF.Identity, bias=b, scale=sc),
                    waits=wn, inc="sS")
        norm_marks[(ring_key, i)] = vn
        v4_marks[(ring_key, i)] = v4
        return vn

    tr_half_ev = {0: None, 1: None}

    def emit_transposes(i, ring_key, dst_fn, xsrc, extra_ev_waits=()):
        par = i % 2
        vn = norm_marks[(ring_key, i)]
        ev = None
        for qg in range(4):
            hh = qg % 2
            w = [("sS", vn)]
            if tr_half_ev[hh]:
                w.append(("sV", tr_half_ev[hh]))
            if ring_key == "b" and i == 0 and qg == 0:
                w.append(("sV", mark_p2_yev))
            for j in range(4):
                dc = 4 * qg + j
                mmv = P.emit("tensor",
                             lambda e, o=pU[:, 512 * hh + 128 * j:
                                            512 * hh + 128 * j + 128],
                             a=xsrc[:, par, 128 * dc:128 * dc + 128]: e.matmul(
                                 o, a, ident_sb[:, :], start=True, stop=True),
                             waits=w if j == 0 else [],
                             inc="sT" if j == 3 else None)
            w2 = [("sT", mmv)] + list(extra_ev_waits)
            ev = P.emit("vector",
                        lambda e, o=dst_fn(qg),
                        a=pU[:, 512 * hh:512 * hh + 512].rearrange(
                            "p (j c) -> p j c", c=128): e.tensor_copy(o, a),
                        waits=w2, inc="sV")
            tr_half_ev[hh] = ev
        tr_marks[(ring_key, i)] = mmv
        ev_marks[(ring_key, i)] = ev
        return ev

    qk_ev = {}
    v_ev = {}
    last_group_mm = 0

    def emit_qkv_group(tc, g):
        nonlocal last_group_mm
        cp = tc % 2
        last_ev = ev_marks[("a", 4 * tc + 3)]
        if g < 8:
            qtr = g % 4
            third = 0 if g < 4 else 1
            w = [("sV", last_ev), ("sWq%d" % third, v_wq3[third])]
            pk = qk_ev.get((tc, g - 4)) or qk_ev.get((tc - 1, g + 4))
            if pk:
                w.append(("sS", pk))
            for dc in range(DC):
                mmv = P.emit("tensor",
                             lambda e, o=pA[:, 512 * qtr:512 * qtr + 512],
                             lh=wqkv_sb[:, dc, g, :], rh=xfm[:, cp, dc, :],
                             st=(dc == 0), sp=(dc == DC - 1): e.matmul(
                                 o, lh, rh, start=st, stop=sp),
                             waits=w if dc == 0 else [],
                             inc="sT" if dc == DC - 1 else None)
            last_group_mm = mmv
            dst = q_sb if g < 4 else k_sb
            jj = g % 4
            qk_ev[(tc, g)] = P.emit(
                "scalar",
                lambda e, o=dst[:, jj, 512 * tc:512 * tc + 512],
                a=pA[:, 512 * qtr:512 * qtr + 512], b=bqt_sb[:, g:g + 1]: e.activation(
                    o, a, AF.Identity, bias=b),
                waits=[("sT", mmv)], inc="sS")
        else:
            vts = g - 8
            vh = vts % 2
            gi = 4 * tc + vts
            w = [("sV", last_ev), ("sWq2", v_wq3[2])]
            pv = v_ev.get(gi - 2)
            if pv:
                w.append(("sV", pv))
            if gi == 0:
                w.append(("sS", v_bvb))
            for dc in range(DC):
                mmv = P.emit("tensor",
                             lambda e, o=pB[:, 512 * vh:512 * vh + 512],
                             lh=xfm[:, cp, dc, 128 * vts:128 * vts + 128],
                             rh=wqkv_sb[:, dc, 8:12, :].rearrange("p a b -> p (a b)"),
                             st=(dc == 0), sp=(dc == DC - 1): e.matmul(
                                 o, lh, rh, start=st, stop=sp),
                             waits=w if dc == 0 else [],
                             inc="sT" if dc == DC - 1 else None)
            last_group_mm = mmv
            v_ev[gi] = P.emit(
                "vector",
                lambda e, o=v_sb[:, gi, :], a=pB[:, 512 * vh:512 * vh + 512]:
                e.tensor_tensor(o, a, bvb[:, :], ALU.add),
                waits=[("sT", mmv)], inc="sV")

    def p1_ln(tc, ts):
        i = 4 * tc + ts
        emit_ln_tile(i, x_seq[128 * i:128 * i + 128, :], xr, "a", xnt)

    def p1_tr(tc, ts):
        i = 4 * tc + ts
        ew = [("sT", last_group_mm)] if tc >= 2 else []
        if tc >= 2:
            ew.append(("sQxn", 16 * tc))
        emit_transposes(
            i, "a",
            lambda qg, cp=tc % 2, ts=ts: xfm[:, cp, 4 * qg:4 * qg + 4,
                                             128 * ts:128 * ts + 128],
            xnt, ew)

    def xn_spill(tc):
        P.dma("sync",
              lambda e, o=xn_all.ap()[tc], a=xfm[:, tc % 2, :, :]:
              e.dma_start(out=o, in_=a),
              waits=[("sV", ev_marks[("a", 4 * tc + 3)])], sem="sQxn")

    for tc in range(9):
        if tc < 8:
            p1_ln(tc, 0)
            p1_ln(tc, 1)
        if tc >= 1:
            xn_spill(tc - 1)
        if tc == 0:
            p1_tr(0, 0); p1_ln(0, 2)
            p1_tr(0, 1); p1_ln(0, 3)
            p1_tr(0, 2); p1_tr(0, 3)
            continue
        nts = 0
        for g in range(12):
            emit_qkv_group(tc - 1, g)
            if tc < 8 and g % 3 == 2 and nts < 4:
                p1_tr(tc, nts)
                if nts + 2 <= 3:
                    p1_ln(tc, nts + 2)
                nts += 1

    mark_p1_mm = last_group_mm
    p1_qk_done = max(qk_ev.values())
    p1_v_done = max(v_ev.values())

    tap("q", q_sb[:, :, :], [("sS", p1_qk_done)])
    tap("k", k_sb[:, :, :], [("sS", p1_qk_done)])
    tap("v", v_sb[:, :, :], [("sV", p1_v_done)])

    v_masks = P.dma("sync",
                    lambda e: e.dma_start(
                        out=masks_sb[:, :, :].rearrange("p a b -> p (a b)"),
                        in_=masks_dr.ap()),
                    waits=[("sT", mark_p1_mm)], sem="sQc")

    def xnr_switch(e, p):
        pid = e.partition_id()
        for arm in e.Switch(pid, n=8):
            for half in range(2):
                e.dma_start(out=xnown[:, :, 512 * half:512 * half + 512],
                            in_=xn_all.ap()[2 * (arm % 4) + half]
                            ).then_inc(p.sems["sQxr"], 16)

    v_xnr = P.emit("sync", xnr_switch,
                   waits=[("sT", mark_p1_mm), ("sQxn", 16 * 8),
                          ("sS", norm_marks[("a", NSUB - 1)])],
                   inc="sQxr", add=32, raw=True)
    tap("xnown", xnown[:, :, :], [("sQxr", v_xnr)])

    mask_last = [0]
    exp_half_last = {0: None, 1: None}
    ptile_av_last = {0: None, 1: None}
    yev = {}
    tail2 = {}
    ny = 0

    pair_list = []
    for h in range(NHL):
        for qt in range(8):
            npair = 2 * (qt + 1)
            for pr in range(npair):
                pair_list.append((h, qt, pr, npair))

    def p2_emit_st_exp(h, qt, pr):
        ph = pr % 2
        w0 = [("sS", p1_qk_done), ("sV", p1_v_done)]
        if exp_half_last[ph]:
            w0.append(("sS", exp_half_last[ph]))
        st_mms = []
        for kk in range(2):
            kc = 2 * pr + kk
            st_mms.append(P.emit(
                "tensor",
                lambda e, o=pA[:, 1024 * ph + 512 * kk:1024 * ph + 512 * kk + 512],
                lh=k_sb[:, h, 128 * kc:128 * kc + 128],
                rh=q_sb[:, h, 512 * qt:512 * qt + 512]: e.matmul(
                    o, lh, rh, start=True, stop=True),
                waits=w0 if kk == 0 else [], inc="sT"))
        mw = []
        for kk in range(2):
            kc = 2 * pr + kk
            o = kc - 4 * qt
            if 0 <= o <= 3:
                mv = P.emit(
                    "vector",
                    lambda e,
                    d=pA[:, 1024 * ph + 512 * kk:1024 * ph + 512 * kk + 512],
                    m=masks_sb[:, o, :]: e.tensor_tensor(d, d, m, ALU.add),
                    waits=[("sT", st_mms[kk]), ("sQc", v_masks)], inc="sV")
                mask_last[0] = mv
                mw.append(("sV", mv))
        slot = pr % 2
        wexp = [("sT", st_mms[1]), ("sV", v_shift)] + mw
        if ptile_av_last[slot]:
            wexp.append(("sT", ptile_av_last[slot]))
        ev = P.emit(
            "scalar",
            lambda e,
            o=ptile[:, 2 * slot:2 * slot + 2, :].rearrange("p a b -> p (a b)"),
            a=pA[:, 1024 * ph:1024 * ph + 1024]: e.activation(
                o, a, AF.Exp, bias=shift_sb[:, :], scale=SM_SCALE),
            waits=wexp, inc="sS")
        exp_half_last[ph] = ev
        return ev

    def p2_emit_av(h, qt, pr, ev):
        u = 8 * h + qt
        pi = u % 2
        yb = pB if pi == 0 else pU
        slot = pr % 2
        for kk in range(2):
            kc = 2 * pr + kk
            wav = []
            if kk == 0:
                wav = [("sS", ev)]
                if kc == 0 and u >= 2:
                    wav += [("sV", yev[u - 2])]
            P.emit("tensor",
                   lambda e, o=yb[:, 0:512], lh=v_sb[:, kc, 128 * h:128 * h + 128],
                   rh=ptile[:, 2 * slot + kk, :],
                   st=(kc == 0), sp=(kc == 4 * qt + 3): e.matmul(
                       o, lh, rh, start=st, stop=sp),
                   waits=wav, inc=None)
            mmv = P.emit("tensor",
                         lambda e, o=yb[:, 512:1024], lh=ones_kk[:, :],
                         rh=ptile[:, 2 * slot + kk, :],
                         st=(kc == 0), sp=(kc == 4 * qt + 3): e.matmul(
                             o, lh, rh, start=st, stop=sp),
                         waits=[], inc="sT")
        ptile_av_last[slot] = mmv
        return mmv

    def p2_tail_a(u, last_av):
        pi = u % 2
        yb = pB if pi == 0 else pU
        t1 = P.emit("vector",
                    lambda e, o=sumtmp[:, pi, :], a=yb[:, 512:1024]:
                    e.tensor_scalar(o, a, SUM_EPS, None, ALU.add),
                    waits=[("sT", last_av)], inc="sV")
        t2 = P.emit("vector",
                    lambda e, o=inv_sb[:, pi, :], a=sumtmp[:, pi, :]:
                    e.reciprocal(o, a),
                    waits=[("sV", t1)], inc="sV")
        tail2[u] = (t2, last_av)

    def p2_tail_b(u, h, qt):
        nonlocal ny
        pi = u % 2
        yb = pB if pi == 0 else pU
        t2, last_av = tail2.pop(u)
        wy = [("sV", t2), ("sT", last_av)]
        if ny >= 2:
            for hh in range(NHL):
                c = min(max(ny - 8 * hh, 0), 8)
                if c > 0:
                    wy.append(("sYh%d" % hh, 16 * c))
        t5 = P.emit("vector",
                    lambda e, o=ynorm[:, pi, :], a=yb[:, 0:512],
                    b=inv_sb[:, pi, :]: e.tensor_tensor(o, a, b, ALU.mult),
                    waits=wy, inc="sV")
        yev[u] = t5
        ny += 1
        P.dma("sync",
              lambda e, o=ag_in.ap()[128 * h:128 * h + 128, 512 * qt:512 * qt + 512],
              a=ynorm[:, pi, :]: e.dma_start(out=o, in_=a),
              waits=[("sV", t5)], sem="sYh%d" % h)

    def emit_ag(l):
        P.emit("gpsimd",
               lambda e, p, l_=l: e.collective_compute(
                   "AllGather", ALU.bypass,
                   replica_groups=[[0, 1, 2, 3], [4, 5, 6, 7]],
                   ins=[ag_in.ap()[128 * l_:128 * l_ + 128, :].opt()],
                   outs=[ag_outh[l_].ap().opt()]
               ).then_inc(p.sems["sCC"], 1),
               waits=[("sYh%d" % l, 16 * 8)], inc="sCC", raw=True)

    pend_av = None
    pend_tb = []
    ag_next = 0
    for (h, qt, pr, npair) in pair_list:
        while ag_next < h:
            emit_ag(ag_next)
            ag_next += 1
        ev = p2_emit_st_exp(h, qt, pr)
        for e_ in pend_tb:
            e_[3] -= 1
        while pend_tb and pend_tb[0][3] <= 0:
            uu, hh, qq, _ = pend_tb.pop(0)
            p2_tail_b(uu, hh, qq)
        if pend_av is not None:
            ph_, qh_, pp_, pe_, last_ = pend_av
            mmv = p2_emit_av(ph_, qh_, pp_, pe_)
            if last_:
                uu = 8 * ph_ + qh_
                p2_tail_a(uu, mmv)
                pend_tb.append([uu, ph_, qh_, 2])
        pend_av = (h, qt, pr, ev, pr == npair - 1)
    ph_, qh_, pp_, pe_, last_ = pend_av
    mmv = p2_emit_av(ph_, qh_, pp_, pe_)
    p2_tail_a(8 * ph_ + qh_, mmv)
    pend_tb.append([8 * ph_ + qh_, ph_, qh_, 0])
    for uu, hh, qq, _ in pend_tb:
        p2_tail_b(uu, hh, qq)

    mark_p2_mm = P.cnt["sT"]
    mark_p2_yev = max(yev.values())
    mark_p2_exp = max(exp_half_last.values())
    mark_p2_mask = mask_last[0]


    v_wr = P.dma("gpsimd",
                 lambda e: e.dma_start(
                     out=wr_sb[:, :].rearrange("p (a e) -> p a e", e=E),
                     in_=wr_t.rearrange("(a p) e -> p a e", p=128)),
                 sem="sWc")
    v_c3 = 0
    for fn in (lambda e: e.dma_start(out=rbiasb[0:1, :], in_=rbias_row[:, :]),
               lambda e: e.dma_start(out=bprojb[0:1, :], in_=bproj_row[:, :]),
               lambda e: e.dma_start(out=onehot_sb[:, :], in_=onehot_dr.ap())):
        v_c3 = P.dma("sync", fn, waits=[("sT", mark_p1_mm)], sem="sQc")
    w_ps = [("sT", mark_p2_mm), ("sQc", v_c3), ("sS", mark_p2_exp),
            ("sV", mark_p2_mask)]
    t1 = P.emit("tensor",
                lambda e: e.matmul(pA[:, 1536:1552], ones_r[:, :], rbiasb[0:1, :],
                                   start=True, stop=True),
                waits=w_ps, inc="sT")
    v_rb = P.emit("scalar", lambda e: e.copy(rbiasb[:, :], pA[:, 1536:1552]),
                  waits=[("sT", t1)], inc="sS")
    bp_ev = []
    for gph in range(4):
        t1 = P.emit("tensor",
                    lambda e, g=gph: e.matmul(pA[:, 512 * g:512 * g + 512], ones_r[:, :],
                                              bprojb[0:1, 512 * g:512 * g + 512],
                                              start=True, stop=True),
                    waits=w_ps + [("sS", v_rb)], inc="sT")
        bp_ev.append(P.emit("scalar",
                            lambda e, g=gph: e.copy(bprojb[:, 512 * g:512 * g + 512],
                                                    pA[:, 512 * g:512 * g + 512]),
                            waits=[("sT", t1)], inc="sS"))
    v_bproj = max(bp_ev)

    rl_ev = {}
    wT_tr = {}

    def emit_router_ts(tsb):
        pq = tsb % 2
        w = [("sQxr", v_xnr), ("sWc", v_wr), ("sS", v_rb), ("sS", v_bproj)]
        if tsb >= 2:
            w.append(("sV", rl_ev[tsb - 2]))
        for dc in range(DC):
            mmv = P.emit("tensor",
                         lambda e, o=pA[:, 512 * pq:512 * pq + 16],
                         lh=xnown[:, dc, 128 * tsb:128 * tsb + 128],
                         rh=wr_sb[:, 16 * dc:16 * dc + 16],
                         st=(dc == 0), sp=(dc == DC - 1): e.matmul(
                             o, lh, rh, start=st, stop=sp),
                         waits=w if dc == 0 else [],
                         inc="sT" if dc == DC - 1 else None)
        L = rt[:, 4 * pq + 0, :]; p_ = rt[:, 4 * pq + 1, :]
        eqa = rt[:, 4 * pq + 2, :]; pm = rt[:, 4 * pq + 3, :]
        s = 8 * pq
        negm = lnt[:, s + 0:s + 1]; m1 = lnt[:, s + 1:s + 2]
        m2 = lnt[:, s + 2:s + 3]; ssum = lnt[:, s + 3:s + 4]
        den = lnt[:, s + 4:s + 5]; w1 = lnt[:, s + 5:s + 6]
        w2 = lnt[:, s + 6:s + 7]
        wl = [("sT", mmv)]
        wl.append(("sS", norm_marks[("a", NSUB - 1)]))
        a1 = P.emit("vector",
                    lambda e, o=L, a=pA[:, 512 * pq:512 * pq + 16]: e.tensor_tensor(
                        o, a, rbiasb[:, :], ALU.add),
                    waits=wl, inc="sV")
        a2m = P.emit("vector", lambda e, o=negm, a=L: e.reduce_max(o, a, axis=AX.X),
                     waits=[("sV", a1)], inc="sV")
        a3 = P.emit("vector",
                    lambda e, o=negm: e.tensor_scalar(o, o, -1.0, None, ALU.mult),
                    waits=[("sV", a2m)], inc="sV")
        a4 = P.emit("scalar",
                    lambda e, o=p_, a=L, b=negm: e.activation(o, a, AF.Exp, bias=b),
                    waits=[("sV", a3)], inc="sS")
        a5 = P.emit("vector", lambda e, o=ssum, a=p_: e.reduce_sum(o, a, axis=AX.X),
                    waits=[("sS", a4)], inc="sV")
        a6 = P.emit("vector", lambda e, o=m1, a=p_: e.reduce_max(o, a, axis=AX.X),
                    waits=[("sS", a4)], inc="sV")
        a7 = P.emit("vector",
                    lambda e, o=eqa, a=p_, b=m1: e.tensor_scalar(o, a, b, None,
                                                                 ALU.is_equal),
                    waits=[("sV", a6)], inc="sV")
        a8 = P.emit("vector",
                    lambda e, o=pm, a=eqa: e.tensor_scalar(o, a, -1.0, 1.0,
                                                           ALU.mult, ALU.add),
                    waits=[("sV", a7)], inc="sV")
        a9 = P.emit("vector",
                    lambda e, o=pm, a=pm, b=p_: e.tensor_tensor(o, a, b, ALU.mult),
                    waits=[("sV", a8)], inc="sV")
        a10 = P.emit("vector", lambda e, o=m2, a=pm: e.reduce_max(o, a, axis=AX.X),
                     waits=[("sV", a9)], inc="sV")
        a11 = P.emit("vector",
                     lambda e, o=pm, a=pm, b=m2: e.tensor_scalar(o, a, b, None,
                                                                 ALU.is_equal),
                     waits=[("sV", a10)], inc="sV")
        a12 = P.emit("vector",
                     lambda e, o=den, a=ssum, b=m1: e.scalar_tensor_tensor(
                         o, a, RT_EPS, b, ALU.mult, ALU.add),
                     waits=[("sV", a10)], inc="sV")
        a13 = P.emit("vector",
                     lambda e, o=den, a=den, b=m2: e.tensor_tensor(o, a, b, ALU.add),
                     waits=[("sV", a12)], inc="sV")
        a14 = P.emit("vector", lambda e, o=den, a=den: e.reciprocal(o, a),
                     waits=[("sV", a13)], inc="sV")
        a15 = P.emit("vector",
                     lambda e, o=w1, a=m1, b=den: e.tensor_tensor(o, a, b, ALU.mult),
                     waits=[("sV", a14)], inc="sV")
        a16 = P.emit("vector",
                     lambda e, o=w2, a=m2, b=den: e.tensor_tensor(o, a, b, ALU.mult),
                     waits=[("sV", a15)], inc="sV")
        a17 = P.emit("vector",
                     lambda e, o=eqa, a=eqa, b=w1: e.tensor_scalar(o, a, b, None,
                                                                   ALU.mult),
                     waits=[("sV", a16)], inc="sV")
        a18 = P.emit("vector",
                     lambda e, o=pm, a=pm, b=w2: e.tensor_scalar(o, a, b, None,
                                                                 ALU.mult),
                     waits=[("sV", a17)], inc="sV")
        wrw = [("sV", a18)]
        if tsb >= 2:
            wrw.append(("sT", wT_tr[tsb - 2]))
        a19 = P.emit("vector",
                     lambda e, o=rW[:, pq, :], a=eqa, b=pm: e.tensor_tensor(
                         o, a, b, ALU.add),
                     waits=wrw, inc="sV")
        rl_ev[tsb] = a19

    def emit_wT(tsb):
        w = [("sV", rl_ev[tsb])]
        if tsb == 0:
            w.append(("sS", v_bproj))
        wT_tr[tsb] = P.emit("tensor",
                            lambda e, o=pA[0:16, 1024 + 128 * tsb:1152 + 128 * tsb],
                            a=rW[:, tsb % 2, :]: e.matmul(o, a, ident_sb[:, :],
                                                          start=True, stop=True),
                            waits=w, inc="sT")

    silu_m = {}; uw_m = {}; a2_m = {}; wbc_m = {}; dpev = {}
    gu_dma = {}; wd_dma = {}
    trk = {"pA0": None, "pA1": None, "pBU_s": None, "pBU_v": None}

    def gu_srcs(e):
        if e < 0:
            return sWg_in, sWu_in
        return Wg_in[e], Wu_in[e]

    def preload_gu(e, ic, waits):
        par = ic % 2
        sg_, su_ = gu_srcs(e)
        vg = P.dma("gpsimd",
                   lambda e_, a=sg_[:, 128 * ic:128 * ic + 128],
                   b=wgr[:, par, :, :]: e_.dma_start(
                       out=b, in_=a.rearrange("(a p) c -> p a c", p=128)),
                   waits=waits, sem="sWg%d" % par)
        vu = P.dma("gpsimd",
                   lambda e_, a=su_[:, 128 * ic:128 * ic + 128],
                   b=wur[:, par, :, :]: e_.dma_start(
                       out=b, in_=a.rearrange("(a p) c -> p a c", p=128)),
                   waits=[], sem="sWu%d" % par)
        gu_dma[(e, ic)] = (vg, vu)

    def preload_wd(e, hg, slot, waits, dest=None, sem=None):
        src = sWd_in if e < 0 else Wd_in[e]
        dst = wdr[:, slot, :, :] if dest is None else dest
        sn = ("sWd%d" % slot) if sem is None else sem
        v = P.dma(
            "gpsimd",
            lambda e_, a=src[:, 512 * hg:512 * hg + 512], b=dst:
            e_.dma_start(out=b, in_=a.rearrange("(a p) c -> p a c", p=128)),
            waits=waits, sem=sn)
        wd_dma[(e, hg)] = ((lambda icx, d=dst: d[:, icx, :]) if dest is not None
                           else (lambda icx, s=slot: wdr[:, s, icx, :]), sn, v)

    def a2_par(e):
        return (e + 1) % 2

    def emit_gu_glue(ei, e, extra0=(), extra0_u=(), interleave=None):
        for ic in range(IC):
            par = ic % 2
            vg, vu = gu_dma[(e, ic)]
            w = [("sWg%d" % par, vg)] + (list(extra0) if ic == 0 else [])
            if ei == 0 and ic == 0:
                w += [("sV", mark_p2_yev), ("sQxr", v_xnr)]
            ps1 = silu_m.get((ei, ic - 1)) or (silu_m.get((ei - 1, IC - 1)) if ic == 0 else None)
            if ps1:
                w.append(("sS", ps1))
            for tt in range(2):
                for dc in range(DC):
                    gmm = P.emit("tensor",
                                 lambda e_, o=pB[:, 512 * tt:512 * tt + 512],
                                 lh=wgr[:, par, dc, :],
                                 rh=xnown[:, dc, 512 * tt:512 * tt + 512],
                                 st=(dc == 0), sp=(dc == DC - 1): e_.matmul(
                                     o, lh, rh, start=st, stop=sp),
                                 waits=w if (tt == 0 and dc == 0) else [],
                                 inc="sT" if (tt == 1 and dc == DC - 1) else None)
            w = [("sWu%d" % par, vu)] + (list(extra0_u) if ic == 0 else [])
            pu1 = a2_m.get((ei, ic - 1)) or (a2_m.get((ei - 1, IC - 1)) if ic == 0 else None)
            if pu1:
                w.append(("sV", pu1))
            if ei == 0 and ic == 0:
                w += [("sV", mark_p2_yev)]
            for tt in range(2):
                for dc in range(DC):
                    umm = P.emit("tensor",
                                 lambda e_, o=pU[:, 512 * tt:512 * tt + 512],
                                 lh=wur[:, par, dc, :],
                                 rh=xnown[:, dc, 512 * tt:512 * tt + 512],
                                 st=(dc == 0), sp=(dc == DC - 1): e_.matmul(
                                     o, lh, rh, start=st, stop=sp),
                                 waits=w if (tt == 0 and dc == 0) else [],
                                 inc="sT" if (tt == 1 and dc == DC - 1) else None)
            wsl = [("sT", gmm)]
            pa2 = a2_m.get((ei, ic - 1)) or (a2_m.get((ei - 1, IC - 1)) if ic == 0 else None)
            if pa2:
                wsl.append(("sV", pa2))
            sv = P.emit("scalar",
                        lambda e_, a=pB[:, 0:1024]: e_.activation(sg_sb[:, :], a,
                                                                  AF.Silu),
                        waits=wsl, inc="sS")
            silu_m[(ei, ic)] = sv
            if e < 0:
                uv = P.emit("vector",
                            lambda e_, a=pU[:, 0:1024]: e_.tensor_copy(uw_sb[:, :], a),
                            waits=[("sT", umm)], inc="sV")
            else:
                uv = P.emit("vector",
                            lambda e_, a=pU[:, 0:1024], b=wbc_sb[:, e % 2, :]:
                            e_.tensor_tensor(uw_sb[:, :], a, b, ALU.mult),
                            waits=[("sT", umm), ("sS", wbc_m[e])], inc="sV")
            uw_m[(ei, ic)] = uv
            av = P.emit("vector",
                        lambda e_, o=a2[:, a2_par(e), ic, :], a=sg_sb[:, :],
                        b=uw_sb[:, :]: e_.tensor_tensor(o, a, b, ALU.mult),
                        waits=[("sS", sv), ("sV", uv)], inc="sV")
            a2_m[(ei, ic)] = av
            if ic + 2 < IC:
                preload_gu(e, ic + 2, [("sT", umm)])
            if interleave is not None:
                interleave(ic)
        return umm

    def emit_wbc(e, extra=()):
        evs = []
        for half in range(2):
            w = list(extra) + [("sS", v_wT)]
            if trk["pA1"]:
                w.append(trk["pA1"])
            t = P.emit("tensor",
                       lambda e_, o=pA[:, 1024 + 512 * half:1024 + 512 * half + 512],
                       lh=onehot_sb[0:16, 128 * e:128 * e + 128],
                       rh=wT_sb[0:16, 512 * half:512 * half + 512]: e_.matmul(
                           o, lh, rh, start=True, stop=True),
                       waits=w if half == 0 else [], inc="sT")
            evs.append(P.emit("scalar",
                              lambda e_, o=wbc_sb[:, e % 2, 512 * half:512 * half + 512],
                              a=pA[:, 1024 + 512 * half:1024 + 512 * half + 512]:
                              e_.copy(o, a),
                              waits=[("sT", t)], inc="sS"))
        wbc_m[e] = max(evs)
        trk["pA1"] = ("sS", wbc_m[e])

    def emit_dp(ei, e, extra_first=()):
        first = True
        for hg in range(4):
            apf, sn, vwd = wd_dma[(e, hg)]
            for ts in range(8):
                ph = ts % 2
                w = [(sn, vwd), ("sV", a2_m[(ei, IC - 1)])]
                if first:
                    w += list(extra_first)
                    first = False
                key = ("dp_half", ph)
                if trk.get(key):
                    w.append(("sV", trk[key]))
                for icx in range(IC):
                    mmv = P.emit("tensor",
                                 lambda e_, o=pA[:, 512 * ph:512 * ph + 512],
                                 lh=a2[:, a2_par(e), icx, 128 * ts:128 * ts + 128],
                                 rh=apf(icx),
                                 st=(icx == 0), sp=(icx == IC - 1): e_.matmul(
                                     o, lh, rh, start=st, stop=sp),
                                 waits=w if icx == 0 else [],
                                 inc="sT" if icx == IC - 1 else None)
                if e < 0:
                    evv = P.emit("vector",
                                 lambda e_, o=acc[:, ts, 512 * hg:512 * hg + 512],
                                 a=pA[:, 512 * ph:512 * ph + 512],
                                 b=bprojb[:, 512 * hg:512 * hg + 512]: e_.tensor_tensor(
                                     o, a, b, ALU.add),
                                 waits=[("sT", mmv), ("sS", v_bproj)], inc="sV")
                else:
                    evv = P.emit("vector",
                                 lambda e_, o=acc[:, ts, 512 * hg:512 * hg + 512],
                                 a=pA[:, 512 * ph:512 * ph + 512]: e_.tensor_tensor(
                                     o, o, a, ALU.add),
                                 waits=[("sT", mmv)], inc="sV")
                dpev[(ei, hg, ts)] = evv
                trk[("dp_half", ph)] = evv
            nxt = dp_wd_chain.pop(0) if dp_wd_chain else None
            if nxt is not None:
                e2, hg2 = nxt
                preload_wd(e2, hg2, int(sn[-1]), [("sT", mmv)])
        return mmv

    preload_gu(-1, 0, [("sT", mark_p1_mm), ("sQxn", 16 * 8)])
    preload_gu(-1, 1, [])
    emit_ag(3)

    def sh_ileave(ic):
        if ic >= 2:
            emit_wT(ic - 2)
        emit_router_ts(ic)

    sh_gu_last = emit_gu_glue(0, -1, interleave=sh_ileave)
    emit_wT(6)
    emit_wT(7)
    v_wT = P.emit("scalar", lambda e: e.copy(wT_sb[:, :], pA[0:16, 1024:2048]),
                  waits=[("sT", wT_tr[7])], inc="sS")
    tap("wT", wT_sb[:, :], [("sS", v_wT)])
    emit_wbc(0, [("sT", mark_p2_mm), ("sV", mark_p2_yev)])

    preload_wd(-1, 0, 0, [("sV", mark_p2_yev), ("sT", mark_p2_mm)]
               + [("sYh%d" % hh, 16 * 8) for hh in range(NHL)])
    preload_wd(-1, 1, 1, [])
    preload_wd(-1, 2, 2, [])
    dp_wd_chain = [(-1, 3)]
    sh_dp_last = emit_dp(0, -1)
    sh_dp_ev = dpev[(0, 3, 7)]

    def rb_src(h):
        r, l = h // 4, h % 4
        return ag_outh[l].ap()[128 * r:128 * r + 128, :]

    def rb_switch_hi(e, p):
        pid = e.partition_id()
        for arm in e.Switch(pid, n=8):
            off = 1024 * (arm % 4)
            for dc in range(8, DC):
                e.dma_start(out=yown[:, dc, :],
                            in_=rb_src(dc)[:, off:off + 1024]
                            ).then_inc(p.sems["sQyh"], 16)

    def rb_switch_lo(e, p):
        pid = e.partition_id()
        for arm in e.Switch(pid, n=8):
            off = 1024 * (arm % 4)
            for dc in range(8):
                e.dma_start(out=yown[:, dc, :],
                            in_=rb_src(dc)[:, off:off + 1024]
                            ).then_inc(p.sems["sQyo"], 16)

    v_yh = P.emit("sync", rb_switch_hi,
                  waits=[("sCC", 4), ("sT", mark_p2_mm)], inc="sQyh",
                  add=16 * 8, raw=True)
    v_yo = P.emit("sync", rb_switch_lo,
                  waits=[("sCC", 4), ("sT", sh_dp_last)], inc="sQyo",
                  add=16 * 8, raw=True)

    wp_mark = {}

    def load_wproj(qd, waits):
        ring, sname = (wproj_a, "sWpa") if qd % 2 == 0 else (wproj_b, "sWpb")
        w = list(waits)
        for j in range(4):
            dc = 4 * qd + j
            v = P.dma("gpsimd",
                      lambda e, a=w_proj[128 * dc:128 * dc + 128, :], b=ring[:, j, :]:
                      e.dma_start(out=b, in_=a),
                      waits=w if j == 0 else [], sem=sname)
        wp_mark[qd] = v

    load_wproj(2, [("sT", sh_gu_last)])
    load_wproj(3, [("sT", sh_dp_last)])

    prj_last_ev = None
    for qd in (2, 3, 0, 1):
        ring, sname = (wproj_a, "sWpa") if qd % 2 == 0 else (wproj_b, "sWpb")
        for ts in range(8):
            grp = ts % 2
            w = [(sname, wp_mark[qd]),
                 ("sQyh", v_yh) if qd >= 2 else ("sQyo", v_yo)]
            key = ("pj", grp)
            if trk.get(key):
                w.append(("sV", trk[key]))
            elif grp == 0:
                w += [("sS", silu_m[(0, IC - 1)]), ("sV", sh_dp_ev)]
            else:
                w += [("sV", sh_dp_ev), ("sS", wbc_m[0])]
            mms = []
            for gph in range(4):
                if grp == 0:
                    ot = pB[:, 512 * (gph % 2):512 * (gph % 2) + 512] if gph < 2 else \
                        pU[:, 512 * (gph % 2):512 * (gph % 2) + 512]
                else:
                    ot = pA[:, 512 * gph:512 * gph + 512]
                for j in range(4):
                    dc = 4 * qd + j
                    mmv = P.emit("tensor",
                                 lambda e_, o=ot,
                                 lh=yown[:, dc, 128 * ts:128 * ts + 128],
                                 rh=ring[:, j, 512 * gph:512 * gph + 512],
                                 st=(j == 0), sp=(j == 3): e_.matmul(
                                     o, lh, rh, start=st, stop=sp),
                                 waits=w if (gph == 0 and j == 0) else [],
                                 inc="sT" if j == 3 else None)
                mms.append(mmv)
            if grp == 0:
                e1 = P.emit("vector",
                            lambda e_, o=acc[:, ts, 0:1024], a=pB[:, 0:1024]:
                            e_.tensor_tensor(o, o, a, ALU.add),
                            waits=[("sT", mms[1])], inc="sV")
                e2 = P.emit("vector",
                            lambda e_, o=acc[:, ts, 1024:2048], a=pU[:, 0:1024]:
                            e_.tensor_tensor(o, o, a, ALU.add),
                            waits=[("sT", mms[3])], inc="sV")
            else:
                e1 = P.emit("vector",
                            lambda e_, o=acc[:, ts, 0:1024], a=pA[:, 0:1024]:
                            e_.tensor_tensor(o, o, a, ALU.add),
                            waits=[("sT", mms[3])], inc="sV")
                e2 = P.emit("vector",
                            lambda e_, o=acc[:, ts, 1024:2048], a=pA[:, 1024:2048]:
                            e_.tensor_tensor(o, o, a, ALU.add),
                            waits=[("sT", mms[3])], inc="sV")
            trk[("pj", grp)] = e2
        if qd == 2:
            load_wproj(0, [("sT", mmv)])
        elif qd == 3:
            load_wproj(1, [("sT", mmv)])
    proj_last_mm = P.cnt["sT"]
    proj_last_ev = max(trk[("pj", 0)], trk[("pj", 1)])
    trk[("dp_half", 0)] = trk[("pj", 1)]
    trk[("dp_half", 1)] = trk[("pj", 1)]
    trk["pA1"] = ("sV", trk[("pj", 1)])

    preload_gu(0, 0, [("sT", proj_last_mm)])
    preload_gu(0, 1, [])
    preload_wd(0, 0, 0, [("sT", proj_last_mm)])
    preload_wd(0, 1, 1, [])
    preload_wd(0, 2, 2, [])
    dp_wd_chain = []
    for e in range(16):
        for hg in range(4):
            if e == 0 and hg < 3:
                continue
            if e == 15 and hg == 3:
                continue
            dp_wd_chain.append((e, hg))

    for e in range(16):
        ei = e + 1
        if e >= 1:
            exw = [("sV", uw_m[(e - 1, IC - 1)])] if e >= 2 else []
            emit_wbc(e, exw)
        ex0 = []
        if e == 0:
            ex0 = [("sT", proj_last_mm), ("sV", trk[("pj", 0)]),
                   ("sV", trk[("pj", 1)])]
        gu_last = emit_gu_glue(ei, e, ex0)
        if e == 15:
            last_gu_mark = gu_last
            preload_wd(15, 3, 0, [("sT", last_gu_mark)], dest=wdx[:, :, :],
                       sem="sWdx")
        if e + 1 < 16:
            preload_gu(e + 1, 0, [("sT", gu_last)])
            preload_gu(e + 1, 1, [])
        if e >= 1:
            emit_dp(ei - 1, e - 1)

    add_m = {}

    def p5_ts(ts):
        par = ts % 2
        xo = xo5a if par == 0 else xo5b
        if ts < 2:
            w = [("sT", last_gu_mark)]
        else:
            w = [("sV", add_m[ts - 2])]
        vx = P.dma("sync",
                   lambda e, a=x_own[128 * ts:128 * ts + 128, :], b=xo[:, :]:
                   e.dma_start(out=b, in_=a),
                   waits=w, sem="sQxo%d" % par)
        wadd = [("sQxo%d" % par, vx)]
        for hg in range(4):
            wadd.append(("sV", dpev[(16, hg, ts)]))
        va = P.emit("vector",
                    lambda e, o=acc[:, ts, :], b=xo[:, :]: e.tensor_tensor(
                        o, o, b, ALU.add),
                    waits=wadd, inc="sV")
        add_m[ts] = va
        P.dma("sync",
              lambda e, o=out_own[128 * ts:128 * ts + 128, :], a=acc[:, ts, :]:
              e.dma_start(out=o, in_=a),
              waits=[("sV", va)], sem="sQst")

    for ts in range(8):
        for hg in range(4):
            apf, sn, vwd = wd_dma[(15, hg)]
            ph = (4 * ts + hg) % 2
            w = [(sn, vwd), ("sV", a2_m[(16, IC - 1)])]
            key = ("dp_half", ph)
            if trk.get(key):
                w.append(("sV", trk[key]))
            for icx in range(IC):
                mmv = P.emit("tensor",
                             lambda e_, o=pA[:, 512 * ph:512 * ph + 512],
                             lh=a2[:, a2_par(15), icx, 128 * ts:128 * ts + 128],
                             rh=apf(icx),
                             st=(icx == 0), sp=(icx == IC - 1): e_.matmul(
                                 o, lh, rh, start=st, stop=sp),
                             waits=w if icx == 0 else [],
                             inc="sT" if icx == IC - 1 else None)
            evv = P.emit("vector",
                         lambda e_, o=acc[:, ts, 512 * hg:512 * hg + 512],
                         a=pA[:, 512 * ph:512 * ph + 512]: e_.tensor_tensor(
                             o, o, a, ALU.add),
                         waits=[("sT", mmv)], inc="sV")
            dpev[(16, hg, ts)] = evv
            trk[("dp_half", ph)] = evv
        p5_ts(ts)
    all_mm = P.cnt["sT"]

    fin = P.cnt["sQst"]
    fw = [("sQst", fin)]
    if P.cnt.get("sQfin"):
        fw.append(("sQfin", P.cnt["sQfin"]))
    P.emit("sync", lambda e: e.nop(), waits=fw)

    nc._dbg = {"xnown": xnown, "wr_sb": wr_sb, "rbiasb": rbiasb, "rt": rt,
               "xnt_b": xnt_b, "lnt": lnt, "q_sb": q_sb, "k_sb": k_sb,
               "v_sb": v_sb, "acc": acc, "xo3": xo3, "wT_sb": wT_sb,
               "a2": a2, "yown": yown, "wbc_sb": wbc_sb, "xfm": xfm}

    with nc.Block() as block:
        @block.sync
        def _(eng):
            P.replay(eng, "sync")

        @block.gpsimd
        def _(eng):
            P.replay(eng, "gpsimd")

        @block.tensor
        def _(eng):
            P.replay(eng, "tensor")

        @block.scalar
        def _(eng):
            P.replay(eng, "scalar")

        @block.vector
        def _(eng):
            P.replay(eng, "vector")

    return nc


def kernel(L0_attn_x, L0_attn_w_attn, L0_attn_b_attn, L0_attn_w_proj,
           L0_attn_b_proj, L0_ffn_Wr, L0_ffn_bias, L0_ffn_Wg, L0_ffn_Wu,
           L0_ffn_Wd, L0_ffn_sWg, L0_ffn_sWu, L0_ffn_sWd):
    x = np.ascontiguousarray(np.asarray(L0_attn_x, dtype=np.float32))
    w_attn = np.asarray(L0_attn_w_attn, dtype=np.float32)
    b_attn = np.asarray(L0_attn_b_attn, dtype=np.float32)
    w_projv = np.ascontiguousarray(np.asarray(L0_attn_w_proj, dtype=np.float32))
    b_proj = np.asarray(L0_attn_b_proj, dtype=np.float32)
    Wr = np.asarray(L0_ffn_Wr, dtype=np.float32)
    rbias = np.asarray(L0_ffn_bias, dtype=np.float32)
    Wg = np.ascontiguousarray(np.asarray(L0_ffn_Wg, dtype=np.float32))
    Wu = np.ascontiguousarray(np.asarray(L0_ffn_Wu, dtype=np.float32))
    Wd = np.ascontiguousarray(np.asarray(L0_ffn_Wd, dtype=np.float32))
    sWg = np.ascontiguousarray(np.asarray(L0_ffn_sWg, dtype=np.float32))
    sWu = np.ascontiguousarray(np.asarray(L0_ffn_sWu, dtype=np.float32))
    sWd = np.ascontiguousarray(np.asarray(L0_ffn_sWd, dtype=np.float32))

    nc = build_program()

    in_maps = []
    for c in range(8):
        s, q4 = c // 4, c % 4
        wq = w_attn[:, 512 * q4:512 * (q4 + 1)]
        wk = w_attn[:, 2048 + 512 * q4:2048 + 512 * (q4 + 1)]
        wv = w_attn[:, 4096 + 512 * q4:4096 + 512 * (q4 + 1)]
        bq = b_attn[512 * q4:512 * (q4 + 1)]
        bk = b_attn[2048 + 512 * q4:2048 + 512 * (q4 + 1)]
        bv = b_attn[4096 + 512 * q4:4096 + 512 * (q4 + 1)]
        in_maps.append({
            "x_seq": np.ascontiguousarray(x[s]),
            "x_own": np.ascontiguousarray(x[s, 1024 * q4:1024 * (q4 + 1)]),
            "wqkv": np.ascontiguousarray(np.concatenate([wq, wk, wv], axis=1)),
            "bqt_in": np.ascontiguousarray(
                np.concatenate([bq.reshape(4, 128).T, bk.reshape(4, 128).T], axis=1)),
            "bv_row": np.ascontiguousarray(bv[None, :]),
            "w_proj": w_projv,
            "bproj_row": np.ascontiguousarray(b_proj[None, :]),
            "wr_t": np.ascontiguousarray(Wr.T),
            "rbias_row": np.ascontiguousarray(rbias[None, :]),
            "Wg_in": Wg, "Wu_in": Wu, "Wd_in": Wd,
            "sWg_in": sWg, "sWu_in": sWu, "sWd_in": sWd,
        })

    res = run_bass_kernel_spmd(nc, in_maps, core_ids=list(range(8)))
    out = np.empty((2, 4096, 2048), dtype=np.float32)
    for c in range(8):
        s, q4 = c // 4, c % 4
        out[s, 1024 * q4:1024 * (q4 + 1)] = res.results[c]["out_own"]
    return out


# version 11
# speedup vs baseline: 1.0453x; 1.0109x over previous
import numpy as np
import ml_dtypes

import concourse.bass as bass
import concourse.mybir as mybir
from concourse.bass_utils import run_bass_kernel_spmd

F32 = mybir.dt.float32
BF16 = mybir.dt.bfloat16
AF = mybir.ActivationFunctionType
ALU = mybir.AluOpType
AX = mybir.AxisListType
BF = ml_dtypes.bfloat16

D = 2048
DC = 16
TSEQ = 4096
TOWN = 1024
NHL = 4
E = 16
DI = 1024
IC = 8
SM_SCALE = 1.0 / float(np.sqrt(128.0))
SM_SHIFT = -24.0
NEG_BIG = -1.0e9
LN_EPS = 1e-5
RT_EPS = 1e-9
SUM_EPS = 1e-30

ENGS = ("sync", "gpsimd", "tensor", "scalar", "vector")


class Plan:
    def __init__(self, nc):
        self.nc = nc
        self.ops = {e: [] for e in ENGS}
        self.cnt = {}
        self.sems = {}

    def sem(self, name):
        if name not in self.sems:
            self.sems[name] = self.nc.alloc_semaphore(name)
            self.cnt[name] = 0
        return self.sems[name]

    def emit(self, eng, fn, waits=(), inc=None, add=1, raw=False):
        val = None
        if inc is not None:
            self.sem(inc)
            self.cnt[inc] += add
            val = self.cnt[inc]
        w = tuple((s, v) for (s, v) in waits if v is not None and v > 0)
        for s, _ in w:
            self.sem(s)
        self.ops[eng].append((fn, w, inc, add, raw))
        return val

    def dma(self, qeng, fn, waits=(), sem=None):
        return self.emit(qeng, fn, waits, inc=sem, add=16)

    def replay(self, eng_obj, name):
        hw = {}
        for fn, waits, inc, add, raw in self.ops[name]:
            for s, v in waits:
                if hw.get(s, 0) < v:
                    eng_obj.wait_ge(self.sems[s], v)
                    hw[s] = v
            if raw:
                fn(eng_obj, self)
            else:
                ins = fn(eng_obj)
                if inc is not None:
                    ins.then_inc(self.sems[inc], add)


def build_program(taps=None):
    taps = set(taps or ())
    nc = bass.Bass("TRN2", target_bir_lowering=False, debug=False)
    P = Plan(nc)

    def dri(n, s, dt=F32):
        return nc.dram_tensor(n, s, dt, kind="ExternalInput").ap()

    x_seq = dri("x_seq", [TSEQ, D])
    x_own = dri("x_own", [TOWN, D])
    wqkv = dri("wqkv", [D, 1536])
    bqt_in = dri("bqt_in", [128, 8])
    bv_row = dri("bv_row", [1, 512])
    w_proj = dri("w_proj", [D, D])
    bproj_row = dri("bproj_row", [1, D])
    wr_t = dri("wr_t", [D, E])
    rbias_row = dri("rbias_row", [1, E])
    Wg_in = dri("Wg_in", [E, D, DI])
    Wu_in = dri("Wu_in", [E, D, DI])
    Wd_in = dri("Wd_in", [E, DI, D])
    sWg_in = dri("sWg_in", [D, DI])
    sWu_in = dri("sWu_in", [D, DI])
    sWd_in = dri("sWd_in", [DI, D])

    out_own = nc.dram_tensor("out_own", [TOWN, D], F32, kind="ExternalOutput").ap()

    ag_in = nc.dram_tensor("ag_in", [512, TSEQ], BF16, kind="Internal")
    ag_outh = [nc.dram_tensor("ag_outh%d" % l, [512, TSEQ], BF16, kind="Internal")
               for l in range(NHL)]
    xn_all = nc.dram_tensor("xn_all", [8, 128, DC, 512], BF16, kind="Internal")

    ident_dr = nc.inline_tensor(np.eye(128, dtype=BF), name="c_ident")
    mk = np.zeros((128, 4, 512), dtype=np.float32)
    for o in range(4):
        kkk = np.arange(128)[:, None] + 128 * o
        qqq = np.arange(512)[None, :]
        mk[:, o, :] = np.where(kkk > qqq, NEG_BIG, 0.0)
    masks_dr = nc.inline_tensor(mk.astype(BF).reshape(128, 4 * 512), name="c_masks")
    ones_k_dr = nc.inline_tensor(np.ones((128, 1), dtype=BF), name="c_onesk")
    ones_kk_dr = nc.inline_tensor(np.ones((128, 128), dtype=BF), name="c_oneskk")
    ones_r_dr = nc.inline_tensor(np.ones((1, 128), dtype=np.float32), name="c_onesr")
    oh = np.zeros((16, 16, 128), dtype=np.float32)
    for ee in range(16):
        oh[ee, ee, :] = 1.0
    onehot_dr = nc.inline_tensor(oh.astype(BF).reshape(16, 16 * 128), name="c_onehot")

    LO, HI = 18432, 229344
    S0 = LO
    S1 = LO + 98304
    S2 = S1 + 49152
    S3 = S2 + 49152

    def at(name, shape, dtype, off):
        esz = 2 if dtype == BF16 else 4
        n = 1
        for s in shape[1:]:
            n *= s
        assert off + n * esz <= HI, (name, off, n * esz)
        return nc.alloc_sbuf_tensor_at(name, shape, dtype, offset=off)

    q_sb = at("q_sb", [128, NHL, TSEQ], BF16, S0)
    k_sb = at("k_sb", [128, NHL, TSEQ], BF16, S0 + 32768)
    v_sb = at("v_sb", [128, 32, 512], BF16, S0 + 65536)
    acc = at("acc", [128, 8, D], F32, S0)
    a2 = at("a2", [128, 2, IC, TOWN], BF16, S0 + 65536)
    yown = at("yown", [128, DC, TOWN], BF16, S0 + 65536)

    wqkv_sb = at("wqkv_sb", [128, DC, 12, 128], BF16, S1)
    masks_sb = at("masks_sb", [128, 4, 512], BF16, S1)
    bprojb = at("bprojb", [128, D], F32, S1 + 4096)
    ptile = at("ptile", [128, 4, 512], BF16, S1 + 12288)
    ynorm = at("ynorm", [128, 2, 512], BF16, S1 + 16384)
    inv_sb = at("inv_sb", [128, 2, 512], F32, S1 + 18432)
    sumtmp = at("sumtmp", [128, 2, 512], F32, S1 + 22528)
    xo3 = at("xo3", [128, 2, D], BF16, S1 + 12288)
    xnt_b = at("xnt_b", [128, 2, D], BF16, S1 + 20480)
    wdr = at("wdr", [128, 3, IC, 512], BF16, S1 + 12288)
    wproj_b = at("wproj_b", [128, 4, D], BF16, S1 + 12288)
    wT_sb = at("wT_sb", [16, 1024], BF16, S1 + 36864)
    onehot_sb = at("onehot_sb", [16, 16 * 128], BF16, S1 + 38912)
    wbc_sb = at("wbc_sb", [128, 2, 1024], BF16, S1 + 43008)
    rt = at("rt", [128, 8, 16], F32, S1 + 47104)
    rW = at("rW", [128, 2, 16], BF16, S1 + 47616)
    rbiasb = at("rbiasb", [128, 16], F32, S1 + 47744)

    xr = at("xr", [128, 2, D], BF16, S2)
    xnt = at("xnt", [128, 2, D], BF16, S2 + 8192)
    xfm = at("xfm", [128, 2, DC, 512], BF16, S2 + 16384)
    xnown = at("xnown", [128, DC, TOWN], BF16, S2)
    wgr = at("wgr", [128, 2, DC, 128], BF16, S2 + 32768)
    wur = at("wur", [128, 2, DC, 128], BF16, S2 + 40960)
    wproj_a = at("wproj_a", [128, 4, D], BF16, S2 + 32768)
    wdx = at("wdx", [128, IC, 512], BF16, S2 + 40960)
    xo5a = at("xo5a", [128, D], F32, S2 + 32768)
    xo5b = at("xo5b", [128, D], F32, S1)

    c3 = S3
    ident_sb = at("ident_sb", [128, 128], BF16, c3); c3 += 256
    ones_k = at("ones_k", [128, 1], BF16, c3); c3 += 64
    ones_kk = at("ones_kk", [128, 128], BF16, c3); c3 += 256
    bqt_sb = at("bqt_sb", [128, 8], F32, c3); c3 += 64
    bvb = at("bvb", [128, 512], F32, c3); c3 += 2048
    lnt = at("lnt", [128, 64], F32, c3); c3 += 256
    ones_r = at("ones_r", [1, 128], F32, c3); c3 += 512
    wr_sb = at("wr_sb", [128, DC * E], BF16, c3); c3 += 512
    sg_sb = at("sg_sb", [128, 1024], BF16, c3); c3 += 2048
    uw_sb = at("uw_sb", [128, 1024], BF16, c3); c3 += 2048
    shift_sb = at("shift_sb", [128, 1], F32, c3); c3 += 64
    xr3 = at("xr3", [128, 2048], BF16, c3); c3 += 4096
    assert c3 <= HI, c3

    pA = nc.alloc_psum_tensor("pA", [128, 2048], F32)
    pB = nc.alloc_psum_tensor("pB", [128, 1024], F32)
    pU = nc.alloc_psum_tensor("pU", [128, 1024], F32)

    tap_list = []

    def tap(name, ap, waits):
        if name not in taps:
            return
        t_out = nc.dram_tensor("tap_" + name, list(ap.shape), ap.dtype,
                               kind="ExternalOutput").ap()
        P.dma("sync", lambda e, o=t_out, a=ap: e.dma_start(out=o, in_=a),
              waits=waits, sem="sQfin")
        tap_list.append(name)

    norm_marks = {}
    v4_marks = {}
    tr_marks = {}
    ev_marks = {}
    predma = {}

    P.dma("sync", lambda e: e.dma_start(out=ident_sb[:, :], in_=ident_dr.ap()), sem="sQc")
    P.dma("sync", lambda e: e.dma_start(out=ones_k[:, :], in_=ones_k_dr.ap()), sem="sQc")
    P.dma("sync", lambda e: e.dma_start(out=ones_kk[:, :], in_=ones_kk_dr.ap()), sem="sQc")
    P.dma("sync", lambda e: e.dma_start(out=ones_r[:, :], in_=ones_r_dr.ap()), sem="sQc")
    P.dma("sync", lambda e: e.dma_start(out=bqt_sb[:, :], in_=bqt_in[:, :]), sem="sQc")
    v_c0 = P.dma("sync", lambda e: e.dma_start(out=bvb[0:1, :], in_=bv_row[:, :]),
                 sem="sQc")
    def xslot_ap(i):
        s = i % 3
        return xr[:, s, :] if s < 2 else xr3[:, :]

    for i in range(3):
        predma[("a", i)] = P.dma(
            "gpsimd",
            lambda e, a=x_seq[128 * i:128 * i + 128, :], b=xslot_ap(i):
            e.dma_start(out=b, in_=a), sem="sWx_a%d" % (i % 3))
    v_wq3 = {}
    for third in range(3):
        for dc in range(DC):
            v_wq3[third] = P.dma(
                "gpsimd",
                lambda e, dc=dc, t=third: e.dma_start(
                    out=wqkv_sb[:, dc, 4 * t:4 * t + 4, :],
                    in_=wqkv[128 * dc:128 * dc + 128, 512 * t:512 * t + 512]
                    .rearrange("p (j c) -> p j c", c=128)),
                sem="sWq%d" % third)
    v_bvbmm = P.emit("tensor",
                     lambda e: e.matmul(pB[:, 0:512], ones_r[:, :], bvb[0:1, :],
                                        start=True, stop=True),
                     waits=[("sQc", v_c0)], inc="sT")
    v_bvb = P.emit("scalar", lambda e: e.copy(bvb[:, :], pB[:, 0:512]),
                   waits=[("sT", v_bvbmm)], inc="sS")
    v_shift = P.emit("vector", lambda e: e.memset(shift_sb[:, :], SM_SHIFT), inc="sV")

    NSUB = TSEQ // 128


    def emit_ln_tile(i, src_ap, dstbuf, ring_key, xout):
        par = i % 2
        sem = "sWx_%s%d" % (ring_key, i % 3)
        if (ring_key, i) in predma:
            vx = predma[(ring_key, i)]
        else:
            wd = []
            if i >= 3:
                wd = [("sS", norm_marks[(ring_key, i - 3)])]
            vx = P.dma("gpsimd",
                       lambda e, a=src_ap, b=xslot_ap(i): e.dma_start(out=b, in_=a),
                       waits=wd, sem=sem)
        s = 8 * par
        st1 = lnt[:, s + 0:s + 1]; st2 = lnt[:, s + 1:s + 2]
        m_ = lnt[:, s + 2:s + 3]; var_ = lnt[:, s + 3:s + 4]
        rec_ = lnt[:, s + 4:s + 5]; rstd_ = lnt[:, s + 5:s + 6]
        negms_ = lnt[:, s + 6:s + 7]
        xa = xslot_ap(i)
        prev = norm_marks.get((ring_key, i - 2)) if ring_key == "a" else (
            norm_marks.get(("b", i - 2)) if i >= 2 else norm_marks.get(("a", NSUB - 2 + i)))
        prev_v4 = v4_marks.get((ring_key, i - 2)) if ring_key == "a" else (
            v4_marks.get(("b", i - 2)) if i >= 2 else v4_marks.get(("a", NSUB - 2 + i)))
        w1 = [(sem, vx)]
        if prev:
            w1.append(("sS", prev))
        v1 = P.emit("vector", lambda e, xa=xa, o=st1: e.reduce_sum(o, xa, axis=AX.X),
                    waits=w1, inc="sV")
        trp = tr_marks.get((ring_key, i - 2)) if ring_key == "a" else (
            tr_marks.get(("b", i - 2)) if i >= 2 else tr_marks.get(("a", NSUB - 2 + i)))
        w2 = [(sem, vx)]
        if prev_v4:
            w2.append(("sV", prev_v4))
        if trp:
            w2.append(("sT", trp))
        v2 = P.emit("scalar",
                    lambda e, xa=xa, o=st2, d=xout[:, par, :]: e.activation(
                        d, xa, AF.Square, accum_out=o),
                    waits=w2, inc="sS")
        v3 = P.emit("vector",
                    lambda e, o=m_, a=st1: e.tensor_scalar(o, a, 1.0 / D, None, ALU.mult),
                    waits=[("sV", v1)], inc="sV")
        v4 = P.emit("vector",
                    lambda e, o=var_, a=st2: e.tensor_scalar(o, a, 1.0 / D, None, ALU.mult),
                    waits=[("sS", v2)], inc="sV")
        v5 = P.emit("vector",
                    lambda e, o=var_, a=m_, b=var_: e.scalar_tensor_tensor(
                        o, a, a[:, :], b, ALU.mult, ALU.subtract),
                    waits=[("sV", v4)], inc="sV")
        v6 = P.emit("vector",
                    lambda e, o=var_: e.tensor_scalar(o, var_, -1.0, LN_EPS,
                                                      ALU.mult, ALU.add),
                    waits=[("sV", v5)], inc="sV")
        v7 = P.emit("vector", lambda e, o=rec_, a=var_: e.reciprocal(o, a),
                    waits=[("sV", v6)], inc="sV")
        v8 = P.emit("scalar", lambda e, o=rstd_, a=rec_: e.activation(o, a, AF.Sqrt),
                    waits=[("sV", v7)], inc="sS")
        v9 = P.emit("vector",
                    lambda e, o=negms_, a=m_, b=rstd_: e.tensor_scalar(
                        o, a, b, -1.0, ALU.mult, ALU.mult),
                    waits=[("sS", v8)], inc="sV")
        wn = [("sV", v9)]
        if trp:
            wn.append(("sT", trp))
        vn = P.emit("scalar",
                    lambda e, o=xout[:, par, :], xa=xa, b=negms_, sc=rstd_: e.activation(
                        o, xa, AF.Identity, bias=b, scale=sc),
                    waits=wn, inc="sS")
        norm_marks[(ring_key, i)] = vn
        v4_marks[(ring_key, i)] = v4
        return vn

    tr_half_ev = {0: None, 1: None}

    def emit_transposes(i, ring_key, dst_fn, xsrc, extra_ev_waits=()):
        par = i % 2
        vn = norm_marks[(ring_key, i)]
        ev = None
        for qg in range(4):
            hh = qg % 2
            w = [("sS", vn)]
            if tr_half_ev[hh]:
                w.append(("sV", tr_half_ev[hh]))
            if ring_key == "b" and i == 0 and qg == 0:
                w.append(("sV", mark_p2_yev))
            for j in range(4):
                dc = 4 * qg + j
                mmv = P.emit("tensor",
                             lambda e, o=pU[:, 512 * hh + 128 * j:
                                            512 * hh + 128 * j + 128],
                             a=xsrc[:, par, 128 * dc:128 * dc + 128]: e.matmul(
                                 o, a, ident_sb[:, :], start=True, stop=True),
                             waits=w if j == 0 else [],
                             inc="sT" if j == 3 else None)
            w2 = [("sT", mmv)] + list(extra_ev_waits)
            ev = P.emit("vector",
                        lambda e, o=dst_fn(qg),
                        a=pU[:, 512 * hh:512 * hh + 512].rearrange(
                            "p (j c) -> p j c", c=128): e.tensor_copy(o, a),
                        waits=w2, inc="sV")
            tr_half_ev[hh] = ev
        tr_marks[(ring_key, i)] = mmv
        ev_marks[(ring_key, i)] = ev
        return ev

    qk_ev = {}
    v_ev = {}
    last_group_mm = 0

    def emit_qkv_group(tc, g):
        nonlocal last_group_mm
        cp = tc % 2
        last_ev = ev_marks[("a", 4 * tc + 3)]
        if g < 8:
            qtr = g % 4
            third = 0 if g < 4 else 1
            w = [("sV", last_ev), ("sWq%d" % third, v_wq3[third])]
            pk = qk_ev.get((tc, g - 4)) or qk_ev.get((tc - 1, g + 4))
            if pk:
                w.append(("sS", pk))
            for dc in range(DC):
                mmv = P.emit("tensor",
                             lambda e, o=pA[:, 512 * qtr:512 * qtr + 512],
                             lh=wqkv_sb[:, dc, g, :], rh=xfm[:, cp, dc, :],
                             st=(dc == 0), sp=(dc == DC - 1): e.matmul(
                                 o, lh, rh, start=st, stop=sp),
                             waits=w if dc == 0 else [],
                             inc="sT" if dc == DC - 1 else None)
            last_group_mm = mmv
            dst = q_sb if g < 4 else k_sb
            jj = g % 4
            qk_ev[(tc, g)] = P.emit(
                "scalar",
                lambda e, o=dst[:, jj, 512 * tc:512 * tc + 512],
                a=pA[:, 512 * qtr:512 * qtr + 512], b=bqt_sb[:, g:g + 1]: e.activation(
                    o, a, AF.Identity, bias=b),
                waits=[("sT", mmv)], inc="sS")
        else:
            vts = g - 8
            vh = vts % 2
            gi = 4 * tc + vts
            w = [("sV", last_ev), ("sWq2", v_wq3[2])]
            pv = v_ev.get(gi - 2)
            if pv:
                w.append(("sV", pv))
            if gi == 0:
                w.append(("sS", v_bvb))
            for dc in range(DC):
                mmv = P.emit("tensor",
                             lambda e, o=pB[:, 512 * vh:512 * vh + 512],
                             lh=xfm[:, cp, dc, 128 * vts:128 * vts + 128],
                             rh=wqkv_sb[:, dc, 8:12, :].rearrange("p a b -> p (a b)"),
                             st=(dc == 0), sp=(dc == DC - 1): e.matmul(
                                 o, lh, rh, start=st, stop=sp),
                             waits=w if dc == 0 else [],
                             inc="sT" if dc == DC - 1 else None)
            last_group_mm = mmv
            v_ev[gi] = P.emit(
                "vector",
                lambda e, o=v_sb[:, gi, :], a=pB[:, 512 * vh:512 * vh + 512]:
                e.tensor_tensor(o, a, bvb[:, :], ALU.add),
                waits=[("sT", mmv)], inc="sV")

    def p1_ln(tc, ts):
        i = 4 * tc + ts
        emit_ln_tile(i, x_seq[128 * i:128 * i + 128, :], xr, "a", xnt)

    def p1_tr(tc, ts):
        i = 4 * tc + ts
        ew = [("sT", last_group_mm)] if tc >= 2 else []
        if tc >= 2:
            ew.append(("sQxn", 16 * tc))
        emit_transposes(
            i, "a",
            lambda qg, cp=tc % 2, ts=ts: xfm[:, cp, 4 * qg:4 * qg + 4,
                                             128 * ts:128 * ts + 128],
            xnt, ew)

    def xn_spill(tc):
        P.dma("sync",
              lambda e, o=xn_all.ap()[tc], a=xfm[:, tc % 2, :, :]:
              e.dma_start(out=o, in_=a),
              waits=[("sV", ev_marks[("a", 4 * tc + 3)])], sem="sQxn")

    for tc in range(9):
        if tc < 8:
            p1_ln(tc, 0)
            p1_ln(tc, 1)
        if tc >= 1:
            xn_spill(tc - 1)
        if tc == 0:
            p1_tr(0, 0); p1_ln(0, 2)
            p1_tr(0, 1); p1_ln(0, 3)
            p1_tr(0, 2); p1_tr(0, 3)
            continue
        nts = 0
        for g in range(12):
            emit_qkv_group(tc - 1, g)
            if tc < 8 and g % 3 == 2 and nts < 4:
                p1_tr(tc, nts)
                if nts + 2 <= 3:
                    p1_ln(tc, nts + 2)
                nts += 1

    mark_p1_mm = last_group_mm
    p1_qk_done = max(qk_ev.values())
    p1_v_done = max(v_ev.values())

    tap("q", q_sb[:, :, :], [("sS", p1_qk_done)])
    tap("k", k_sb[:, :, :], [("sS", p1_qk_done)])
    tap("v", v_sb[:, :, :], [("sV", p1_v_done)])

    v_masks = P.dma("sync",
                    lambda e: e.dma_start(
                        out=masks_sb[:, :, :].rearrange("p a b -> p (a b)"),
                        in_=masks_dr.ap()),
                    waits=[("sT", mark_p1_mm)], sem="sQc")
    v_wr = P.dma("gpsimd",
                 lambda e: e.dma_start(
                     out=wr_sb[:, :].rearrange("p (a e) -> p a e", e=E),
                     in_=wr_t.rearrange("(a p) e -> p a e", p=128)),
                 sem="sWc")
    v_c3 = 0
    for fn in (lambda e: e.dma_start(out=rbiasb[0:1, :], in_=rbias_row[:, :]),
               lambda e: e.dma_start(out=bprojb[0:1, :], in_=bproj_row[:, :]),
               lambda e: e.dma_start(out=onehot_sb[:, :], in_=onehot_dr.ap())):
        v_c3 = P.dma("sync", fn, waits=[("sT", mark_p1_mm)], sem="sQc")

    def xnr_switch(e, p):
        pid = e.partition_id()
        for arm in e.Switch(pid, n=8):
            for half in range(2):
                e.dma_start(out=xnown[:, :, 512 * half:512 * half + 512],
                            in_=xn_all.ap()[2 * (arm % 4) + half]
                            ).then_inc(p.sems["sQxr"], 16)

    v_xnr = P.emit("sync", xnr_switch,
                   waits=[("sT", mark_p1_mm), ("sQxn", 16 * 8),
                          ("sS", norm_marks[("a", NSUB - 1)])],
                   inc="sQxr", add=32, raw=True)
    tap("xnown", xnown[:, :, :], [("sQxr", v_xnr)])

    mask_last = [0]
    exp_half_last = {0: None, 1: None}
    ptile_av_last = {0: None, 1: None}
    yev = {}
    tail2 = {}
    ny = 0

    pair_list = []
    for h in range(NHL):
        for qt in range(8):
            npair = 2 * (qt + 1)
            for pr in range(npair):
                pair_list.append((h, qt, pr, npair))

    def p2_emit_st_exp(h, qt, pr):
        ph = pr % 2
        w0 = [("sS", p1_qk_done), ("sV", p1_v_done)]
        if exp_half_last[ph]:
            w0.append(("sS", exp_half_last[ph]))
        st_mms = []
        for kk in range(2):
            kc = 2 * pr + kk
            st_mms.append(P.emit(
                "tensor",
                lambda e, o=pA[:, 1024 * ph + 512 * kk:1024 * ph + 512 * kk + 512],
                lh=k_sb[:, h, 128 * kc:128 * kc + 128],
                rh=q_sb[:, h, 512 * qt:512 * qt + 512]: e.matmul(
                    o, lh, rh, start=True, stop=True),
                waits=w0 if kk == 0 else [], inc="sT"))
        mw = []
        for kk in range(2):
            kc = 2 * pr + kk
            o = kc - 4 * qt
            if 0 <= o <= 3:
                mv = P.emit(
                    "vector",
                    lambda e,
                    d=pA[:, 1024 * ph + 512 * kk:1024 * ph + 512 * kk + 512],
                    m=masks_sb[:, o, :]: e.tensor_tensor(d, d, m, ALU.add),
                    waits=[("sT", st_mms[kk]), ("sQc", v_masks)], inc="sV")
                mask_last[0] = mv
                mw.append(("sV", mv))
        slot = pr % 2
        wexp = [("sT", st_mms[1]), ("sV", v_shift)] + mw
        if ptile_av_last[slot]:
            wexp.append(("sT", ptile_av_last[slot]))
        ev = P.emit(
            "scalar",
            lambda e,
            o=ptile[:, 2 * slot:2 * slot + 2, :].rearrange("p a b -> p (a b)"),
            a=pA[:, 1024 * ph:1024 * ph + 1024]: e.activation(
                o, a, AF.Exp, bias=shift_sb[:, :], scale=SM_SCALE),
            waits=wexp, inc="sS")
        exp_half_last[ph] = ev
        return ev

    def p2_emit_av(h, qt, pr, ev):
        u = 8 * h + qt
        pi = u % 2
        yb = pB if pi == 0 else pU
        slot = pr % 2
        for kk in range(2):
            kc = 2 * pr + kk
            wav = []
            if kk == 0:
                wav = [("sS", ev)]
                if kc == 0 and u >= 2:
                    wav += [("sV", yev[u - 2])]
            P.emit("tensor",
                   lambda e, o=yb[:, 0:512], lh=v_sb[:, kc, 128 * h:128 * h + 128],
                   rh=ptile[:, 2 * slot + kk, :],
                   st=(kc == 0), sp=(kc == 4 * qt + 3): e.matmul(
                       o, lh, rh, start=st, stop=sp),
                   waits=wav, inc=None)
            mmv = P.emit("tensor",
                         lambda e, o=yb[:, 512:1024], lh=ones_kk[:, :],
                         rh=ptile[:, 2 * slot + kk, :],
                         st=(kc == 0), sp=(kc == 4 * qt + 3): e.matmul(
                             o, lh, rh, start=st, stop=sp),
                         waits=[], inc="sT")
        ptile_av_last[slot] = mmv
        return mmv

    def p2_tail_a(u, last_av):
        pi = u % 2
        yb = pB if pi == 0 else pU
        t1 = P.emit("vector",
                    lambda e, o=sumtmp[:, pi, :], a=yb[:, 512:1024]:
                    e.tensor_scalar(o, a, SUM_EPS, None, ALU.add),
                    waits=[("sT", last_av)], inc="sV")
        t2 = P.emit("vector",
                    lambda e, o=inv_sb[:, pi, :], a=sumtmp[:, pi, :]:
                    e.reciprocal(o, a),
                    waits=[("sV", t1)], inc="sV")
        tail2[u] = (t2, last_av)

    def p2_tail_b(u, h, qt):
        nonlocal ny
        pi = u % 2
        yb = pB if pi == 0 else pU
        t2, last_av = tail2.pop(u)
        wy = [("sV", t2), ("sT", last_av)]
        if ny >= 2:
            for hh in range(NHL):
                c = min(max(ny - 8 * hh, 0), 8)
                if c > 0:
                    wy.append(("sYh%d" % hh, 16 * c))
        t5 = P.emit("vector",
                    lambda e, o=ynorm[:, pi, :], a=yb[:, 0:512],
                    b=inv_sb[:, pi, :]: e.tensor_tensor(o, a, b, ALU.mult),
                    waits=wy, inc="sV")
        yev[u] = t5
        ny += 1
        P.dma("gpsimd",
              lambda e, o=ag_in.ap()[128 * h:128 * h + 128, 512 * qt:512 * qt + 512],
              a=ynorm[:, pi, :]: e.dma_start(out=o, in_=a),
              waits=[("sV", t5)], sem="sYh%d" % h)
        if qt == 7 and h < NHL - 1:
            emit_ag(h)

    def emit_ag(l):
        P.emit("gpsimd",
               lambda e, p, l_=l: e.collective_compute(
                   "AllGather", ALU.bypass,
                   replica_groups=[[0, 1, 2, 3], [4, 5, 6, 7]],
                   ins=[ag_in.ap()[128 * l_:128 * l_ + 128, :].opt()],
                   outs=[ag_outh[l_].ap().opt()]
               ).then_inc(p.sems["sCC"], 1),
               waits=[("sYh%d" % l, 16 * 8)], inc="sCC", raw=True)

    pend_av = None
    pend_tb = []
    for (h, qt, pr, npair) in pair_list:
        ev = p2_emit_st_exp(h, qt, pr)
        for e_ in pend_tb:
            e_[3] -= 1
        while pend_tb and pend_tb[0][3] <= 0:
            uu, hh, qq, _ = pend_tb.pop(0)
            p2_tail_b(uu, hh, qq)
        if pend_av is not None:
            ph_, qh_, pp_, pe_, last_ = pend_av
            mmv = p2_emit_av(ph_, qh_, pp_, pe_)
            if last_:
                uu = 8 * ph_ + qh_
                p2_tail_a(uu, mmv)
                pend_tb.append([uu, ph_, qh_, 2])
        pend_av = (h, qt, pr, ev, pr == npair - 1)
    ph_, qh_, pp_, pe_, last_ = pend_av
    mmv = p2_emit_av(ph_, qh_, pp_, pe_)
    p2_tail_a(8 * ph_ + qh_, mmv)
    pend_tb.append([8 * ph_ + qh_, ph_, qh_, 0])
    for uu, hh, qq, _ in pend_tb:
        p2_tail_b(uu, hh, qq)

    mark_p2_mm = P.cnt["sT"]
    mark_p2_yev = max(yev.values())
    mark_p2_exp = max(exp_half_last.values())
    mark_p2_mask = mask_last[0]


    w_ps = [("sT", mark_p2_mm), ("sQc", v_c3), ("sS", mark_p2_exp),
            ("sV", mark_p2_mask)]
    t1 = P.emit("tensor",
                lambda e: e.matmul(pA[:, 1536:1552], ones_r[:, :], rbiasb[0:1, :],
                                   start=True, stop=True),
                waits=w_ps, inc="sT")
    v_rb = P.emit("scalar", lambda e: e.copy(rbiasb[:, :], pA[:, 1536:1552]),
                  waits=[("sT", t1)], inc="sS")
    bp_ev = []
    for gph in range(4):
        t1 = P.emit("tensor",
                    lambda e, g=gph: e.matmul(pA[:, 512 * g:512 * g + 512], ones_r[:, :],
                                              bprojb[0:1, 512 * g:512 * g + 512],
                                              start=True, stop=True),
                    waits=w_ps + [("sS", v_rb)], inc="sT")
        bp_ev.append(P.emit("scalar",
                            lambda e, g=gph: e.copy(bprojb[:, 512 * g:512 * g + 512],
                                                    pA[:, 512 * g:512 * g + 512]),
                            waits=[("sT", t1)], inc="sS"))
    v_bproj = max(bp_ev)

    rl_ev = {}
    wT_tr = {}

    def emit_router_ts(tsb):
        pq = tsb % 2
        w = [("sQxr", v_xnr), ("sWc", v_wr), ("sS", v_rb), ("sS", v_bproj)]
        if tsb >= 2:
            w.append(("sV", rl_ev[tsb - 2]))
        for dc in range(DC):
            mmv = P.emit("tensor",
                         lambda e, o=pA[:, 512 * pq:512 * pq + 16],
                         lh=xnown[:, dc, 128 * tsb:128 * tsb + 128],
                         rh=wr_sb[:, 16 * dc:16 * dc + 16],
                         st=(dc == 0), sp=(dc == DC - 1): e.matmul(
                             o, lh, rh, start=st, stop=sp),
                         waits=w if dc == 0 else [],
                         inc="sT" if dc == DC - 1 else None)
        L = rt[:, 4 * pq + 0, :]; p_ = rt[:, 4 * pq + 1, :]
        eqa = rt[:, 4 * pq + 2, :]; pm = rt[:, 4 * pq + 3, :]
        s = 8 * pq
        negm = lnt[:, s + 0:s + 1]; m1 = lnt[:, s + 1:s + 2]
        m2 = lnt[:, s + 2:s + 3]; ssum = lnt[:, s + 3:s + 4]
        den = lnt[:, s + 4:s + 5]; w1 = lnt[:, s + 5:s + 6]
        w2 = lnt[:, s + 6:s + 7]
        wl = [("sT", mmv)]
        wl.append(("sS", norm_marks[("a", NSUB - 1)]))
        a1 = P.emit("vector",
                    lambda e, o=L, a=pA[:, 512 * pq:512 * pq + 16]: e.tensor_tensor(
                        o, a, rbiasb[:, :], ALU.add),
                    waits=wl, inc="sV")
        a2m = P.emit("vector", lambda e, o=negm, a=L: e.reduce_max(o, a, axis=AX.X),
                     waits=[("sV", a1)], inc="sV")
        a3 = P.emit("vector",
                    lambda e, o=negm: e.tensor_scalar(o, o, -1.0, None, ALU.mult),
                    waits=[("sV", a2m)], inc="sV")
        a4 = P.emit("scalar",
                    lambda e, o=p_, a=L, b=negm: e.activation(o, a, AF.Exp, bias=b),
                    waits=[("sV", a3)], inc="sS")
        a5 = P.emit("vector", lambda e, o=ssum, a=p_: e.reduce_sum(o, a, axis=AX.X),
                    waits=[("sS", a4)], inc="sV")
        a6 = P.emit("vector", lambda e, o=m1, a=p_: e.reduce_max(o, a, axis=AX.X),
                    waits=[("sS", a4)], inc="sV")
        a7 = P.emit("vector",
                    lambda e, o=eqa, a=p_, b=m1: e.tensor_scalar(o, a, b, None,
                                                                 ALU.is_equal),
                    waits=[("sV", a6)], inc="sV")
        a8 = P.emit("vector",
                    lambda e, o=pm, a=eqa: e.tensor_scalar(o, a, -1.0, 1.0,
                                                           ALU.mult, ALU.add),
                    waits=[("sV", a7)], inc="sV")
        a9 = P.emit("vector",
                    lambda e, o=pm, a=pm, b=p_: e.tensor_tensor(o, a, b, ALU.mult),
                    waits=[("sV", a8)], inc="sV")
        a10 = P.emit("vector", lambda e, o=m2, a=pm: e.reduce_max(o, a, axis=AX.X),
                     waits=[("sV", a9)], inc="sV")
        a11 = P.emit("vector",
                     lambda e, o=pm, a=pm, b=m2: e.tensor_scalar(o, a, b, None,
                                                                 ALU.is_equal),
                     waits=[("sV", a10)], inc="sV")
        a12 = P.emit("vector",
                     lambda e, o=den, a=ssum, b=m1: e.scalar_tensor_tensor(
                         o, a, RT_EPS, b, ALU.mult, ALU.add),
                     waits=[("sV", a10)], inc="sV")
        a13 = P.emit("vector",
                     lambda e, o=den, a=den, b=m2: e.tensor_tensor(o, a, b, ALU.add),
                     waits=[("sV", a12)], inc="sV")
        a14 = P.emit("vector", lambda e, o=den, a=den: e.reciprocal(o, a),
                     waits=[("sV", a13)], inc="sV")
        a15 = P.emit("vector",
                     lambda e, o=w1, a=m1, b=den: e.tensor_tensor(o, a, b, ALU.mult),
                     waits=[("sV", a14)], inc="sV")
        a16 = P.emit("vector",
                     lambda e, o=w2, a=m2, b=den: e.tensor_tensor(o, a, b, ALU.mult),
                     waits=[("sV", a15)], inc="sV")
        a17 = P.emit("vector",
                     lambda e, o=eqa, a=eqa, b=w1: e.tensor_scalar(o, a, b, None,
                                                                   ALU.mult),
                     waits=[("sV", a16)], inc="sV")
        a18 = P.emit("vector",
                     lambda e, o=pm, a=pm, b=w2: e.tensor_scalar(o, a, b, None,
                                                                 ALU.mult),
                     waits=[("sV", a17)], inc="sV")
        wrw = [("sV", a18)]
        if tsb >= 2:
            wrw.append(("sT", wT_tr[tsb - 2]))
        a19 = P.emit("vector",
                     lambda e, o=rW[:, pq, :], a=eqa, b=pm: e.tensor_tensor(
                         o, a, b, ALU.add),
                     waits=wrw, inc="sV")
        rl_ev[tsb] = a19

    def emit_wT(tsb):
        w = [("sV", rl_ev[tsb])]
        if tsb == 0:
            w.append(("sS", v_bproj))
        wT_tr[tsb] = P.emit("tensor",
                            lambda e, o=pA[0:16, 1024 + 128 * tsb:1152 + 128 * tsb],
                            a=rW[:, tsb % 2, :]: e.matmul(o, a, ident_sb[:, :],
                                                          start=True, stop=True),
                            waits=w, inc="sT")

    silu_m = {}; uw_m = {}; a2_m = {}; wbc_m = {}; dpev = {}
    gu_dma = {}; wd_dma = {}
    trk = {"pA0": None, "pA1": None, "pBU_s": None, "pBU_v": None}

    def gu_srcs(e):
        if e < 0:
            return sWg_in, sWu_in
        return Wg_in[e], Wu_in[e]

    def preload_gu(e, ic, waits):
        par = ic % 2
        sg_, su_ = gu_srcs(e)
        vg = P.dma("gpsimd",
                   lambda e_, a=sg_[:, 128 * ic:128 * ic + 128],
                   b=wgr[:, par, :, :]: e_.dma_start(
                       out=b, in_=a.rearrange("(a p) c -> p a c", p=128)),
                   waits=waits, sem="sWg%d" % par)
        vu = P.dma("gpsimd",
                   lambda e_, a=su_[:, 128 * ic:128 * ic + 128],
                   b=wur[:, par, :, :]: e_.dma_start(
                       out=b, in_=a.rearrange("(a p) c -> p a c", p=128)),
                   waits=[], sem="sWu%d" % par)
        gu_dma[(e, ic)] = (vg, vu)

    def preload_wd(e, hg, slot, waits, dest=None, sem=None):
        src = sWd_in if e < 0 else Wd_in[e]
        dst = wdr[:, slot, :, :] if dest is None else dest
        sn = ("sWd%d" % slot) if sem is None else sem
        v = P.dma(
            "gpsimd",
            lambda e_, a=src[:, 512 * hg:512 * hg + 512], b=dst:
            e_.dma_start(out=b, in_=a.rearrange("(a p) c -> p a c", p=128)),
            waits=waits, sem=sn)
        wd_dma[(e, hg)] = ((lambda icx, d=dst: d[:, icx, :]) if dest is not None
                           else (lambda icx, s=slot: wdr[:, s, icx, :]), sn, v)

    def a2_par(e):
        return (e + 1) % 2

    def emit_gu_glue(ei, e, extra0=(), extra0_u=(), interleave=None):
        for ic in range(IC):
            par = ic % 2
            vg, vu = gu_dma[(e, ic)]
            w = [("sWg%d" % par, vg)] + (list(extra0) if ic == 0 else [])
            if ei == 0 and ic == 0:
                w += [("sV", mark_p2_yev), ("sQxr", v_xnr)]
            ps1 = silu_m.get((ei, ic - 1)) or (silu_m.get((ei - 1, IC - 1)) if ic == 0 else None)
            if ps1:
                w.append(("sS", ps1))
            for tt in range(2):
                for dc in range(DC):
                    gmm = P.emit("tensor",
                                 lambda e_, o=pB[:, 512 * tt:512 * tt + 512],
                                 lh=wgr[:, par, dc, :],
                                 rh=xnown[:, dc, 512 * tt:512 * tt + 512],
                                 st=(dc == 0), sp=(dc == DC - 1): e_.matmul(
                                     o, lh, rh, start=st, stop=sp),
                                 waits=w if (tt == 0 and dc == 0) else [],
                                 inc="sT" if (tt == 1 and dc == DC - 1) else None)
            w = [("sWu%d" % par, vu)] + (list(extra0_u) if ic == 0 else [])
            pu1 = a2_m.get((ei, ic - 1)) or (a2_m.get((ei - 1, IC - 1)) if ic == 0 else None)
            if pu1:
                w.append(("sV", pu1))
            if ei == 0 and ic == 0:
                w += [("sV", mark_p2_yev)]
            for tt in range(2):
                for dc in range(DC):
                    umm = P.emit("tensor",
                                 lambda e_, o=pU[:, 512 * tt:512 * tt + 512],
                                 lh=wur[:, par, dc, :],
                                 rh=xnown[:, dc, 512 * tt:512 * tt + 512],
                                 st=(dc == 0), sp=(dc == DC - 1): e_.matmul(
                                     o, lh, rh, start=st, stop=sp),
                                 waits=w if (tt == 0 and dc == 0) else [],
                                 inc="sT" if (tt == 1 and dc == DC - 1) else None)
            wsl = [("sT", gmm)]
            pa2 = a2_m.get((ei, ic - 1)) or (a2_m.get((ei - 1, IC - 1)) if ic == 0 else None)
            if pa2:
                wsl.append(("sV", pa2))
            sv = P.emit("scalar",
                        lambda e_, a=pB[:, 0:1024]: e_.activation(sg_sb[:, :], a,
                                                                  AF.Silu),
                        waits=wsl, inc="sS")
            silu_m[(ei, ic)] = sv
            if e < 0:
                uv = P.emit("vector",
                            lambda e_, a=pU[:, 0:1024]: e_.tensor_copy(uw_sb[:, :], a),
                            waits=[("sT", umm)], inc="sV")
            else:
                uv = P.emit("vector",
                            lambda e_, a=pU[:, 0:1024], b=wbc_sb[:, e % 2, :]:
                            e_.tensor_tensor(uw_sb[:, :], a, b, ALU.mult),
                            waits=[("sT", umm), ("sS", wbc_m[e])], inc="sV")
            uw_m[(ei, ic)] = uv
            av = P.emit("vector",
                        lambda e_, o=a2[:, a2_par(e), ic, :], a=sg_sb[:, :],
                        b=uw_sb[:, :]: e_.tensor_tensor(o, a, b, ALU.mult),
                        waits=[("sS", sv), ("sV", uv)], inc="sV")
            a2_m[(ei, ic)] = av
            if ic + 2 < IC:
                preload_gu(e, ic + 2, [("sT", umm)])
            if interleave is not None:
                interleave(ic)
        return umm

    def emit_wbc(e, extra=()):
        evs = []
        for half in range(2):
            w = list(extra) + [("sS", v_wT)]
            if trk["pA1"]:
                w.append(trk["pA1"])
            t = P.emit("tensor",
                       lambda e_, o=pA[:, 1024 + 512 * half:1024 + 512 * half + 512],
                       lh=onehot_sb[0:16, 128 * e:128 * e + 128],
                       rh=wT_sb[0:16, 512 * half:512 * half + 512]: e_.matmul(
                           o, lh, rh, start=True, stop=True),
                       waits=w if half == 0 else [], inc="sT")
            evs.append(P.emit("scalar",
                              lambda e_, o=wbc_sb[:, e % 2, 512 * half:512 * half + 512],
                              a=pA[:, 1024 + 512 * half:1024 + 512 * half + 512]:
                              e_.copy(o, a),
                              waits=[("sT", t)], inc="sS"))
        wbc_m[e] = max(evs)
        trk["pA1"] = ("sS", wbc_m[e])

    def emit_dp(ei, e, extra_first=()):
        first = True
        for hg in range(4):
            apf, sn, vwd = wd_dma[(e, hg)]
            for ts in range(8):
                ph = ts % 2
                w = [(sn, vwd), ("sV", a2_m[(ei, IC - 1)])]
                if first:
                    w += list(extra_first)
                    first = False
                key = ("dp_half", ph)
                if trk.get(key):
                    w.append(("sV", trk[key]))
                for icx in range(IC):
                    mmv = P.emit("tensor",
                                 lambda e_, o=pA[:, 512 * ph:512 * ph + 512],
                                 lh=a2[:, a2_par(e), icx, 128 * ts:128 * ts + 128],
                                 rh=apf(icx),
                                 st=(icx == 0), sp=(icx == IC - 1): e_.matmul(
                                     o, lh, rh, start=st, stop=sp),
                                 waits=w if icx == 0 else [],
                                 inc="sT" if icx == IC - 1 else None)
                if e < 0:
                    evv = P.emit("vector",
                                 lambda e_, o=acc[:, ts, 512 * hg:512 * hg + 512],
                                 a=pA[:, 512 * ph:512 * ph + 512],
                                 b=bprojb[:, 512 * hg:512 * hg + 512]: e_.tensor_tensor(
                                     o, a, b, ALU.add),
                                 waits=[("sT", mmv), ("sS", v_bproj)], inc="sV")
                else:
                    evv = P.emit("vector",
                                 lambda e_, o=acc[:, ts, 512 * hg:512 * hg + 512],
                                 a=pA[:, 512 * ph:512 * ph + 512]: e_.tensor_tensor(
                                     o, o, a, ALU.add),
                                 waits=[("sT", mmv)], inc="sV")
                dpev[(ei, hg, ts)] = evv
                trk[("dp_half", ph)] = evv
            nxt = dp_wd_chain.pop(0) if dp_wd_chain else None
            if nxt is not None:
                e2, hg2 = nxt
                preload_wd(e2, hg2, int(sn[-1]), [("sT", mmv)])
        return mmv

    preload_gu(-1, 0, [("sT", mark_p1_mm), ("sQxn", 16 * 8)])
    preload_gu(-1, 1, [])
    emit_ag(3)

    def sh_ileave(ic):
        if ic >= 2:
            emit_wT(ic - 2)
        emit_router_ts(ic)

    sh_gu_last = emit_gu_glue(0, -1, interleave=sh_ileave)
    emit_wT(6)
    emit_wT(7)
    v_wT = P.emit("scalar", lambda e: e.copy(wT_sb[:, :], pA[0:16, 1024:2048]),
                  waits=[("sT", wT_tr[7])], inc="sS")
    tap("wT", wT_sb[:, :], [("sS", v_wT)])
    emit_wbc(0, [("sT", mark_p2_mm), ("sV", mark_p2_yev)])

    preload_wd(-1, 0, 0, [("sV", mark_p2_yev), ("sT", mark_p2_mm)]
               + [("sYh%d" % hh, 16 * 8) for hh in range(NHL)])
    preload_wd(-1, 1, 1, [])
    preload_wd(-1, 2, 2, [])
    dp_wd_chain = [(-1, 3)]
    sh_dp_last = emit_dp(0, -1)
    sh_dp_ev = dpev[(0, 3, 7)]

    def rb_src(h):
        r, l = h // 4, h % 4
        return ag_outh[l].ap()[128 * r:128 * r + 128, :]

    def rb_switch_hi(e, p):
        pid = e.partition_id()
        for arm in e.Switch(pid, n=8):
            off = 1024 * (arm % 4)
            for dc in range(8, DC):
                e.dma_start(out=yown[:, dc, :],
                            in_=rb_src(dc)[:, off:off + 1024]
                            ).then_inc(p.sems["sQyh"], 16)

    def rb_switch_lo(e, p):
        pid = e.partition_id()
        for arm in e.Switch(pid, n=8):
            off = 1024 * (arm % 4)
            for dc in range(8):
                e.dma_start(out=yown[:, dc, :],
                            in_=rb_src(dc)[:, off:off + 1024]
                            ).then_inc(p.sems["sQyo"], 16)

    v_yh = P.emit("sync", rb_switch_hi,
                  waits=[("sCC", 4), ("sT", mark_p2_mm)], inc="sQyh",
                  add=16 * 8, raw=True)
    v_yo = P.emit("sync", rb_switch_lo,
                  waits=[("sCC", 4), ("sT", sh_dp_last)], inc="sQyo",
                  add=16 * 8, raw=True)

    wp_mark = {}

    def load_wproj(qd, waits):
        ring, sname = (wproj_a, "sWpa") if qd % 2 == 0 else (wproj_b, "sWpb")
        w = list(waits)
        for j in range(4):
            dc = 4 * qd + j
            v = P.dma("gpsimd",
                      lambda e, a=w_proj[128 * dc:128 * dc + 128, :], b=ring[:, j, :]:
                      e.dma_start(out=b, in_=a),
                      waits=w if j == 0 else [], sem=sname)
        wp_mark[qd] = v

    load_wproj(2, [("sT", sh_gu_last)])
    load_wproj(3, [("sT", sh_dp_last)])

    prj_last_ev = None
    for qd in (2, 3, 0, 1):
        ring, sname = (wproj_a, "sWpa") if qd % 2 == 0 else (wproj_b, "sWpb")
        for ts in range(8):
            grp = ts % 2
            w = [(sname, wp_mark[qd]),
                 ("sQyh", v_yh) if qd >= 2 else ("sQyo", v_yo)]
            key = ("pj", grp)
            if trk.get(key):
                w.append(("sV", trk[key]))
            elif grp == 0:
                w += [("sS", silu_m[(0, IC - 1)]), ("sV", sh_dp_ev)]
            else:
                w += [("sV", sh_dp_ev), ("sS", wbc_m[0])]
            mms = []
            for gph in range(4):
                if grp == 0:
                    ot = pB[:, 512 * (gph % 2):512 * (gph % 2) + 512] if gph < 2 else \
                        pU[:, 512 * (gph % 2):512 * (gph % 2) + 512]
                else:
                    ot = pA[:, 512 * gph:512 * gph + 512]
                for j in range(4):
                    dc = 4 * qd + j
                    mmv = P.emit("tensor",
                                 lambda e_, o=ot,
                                 lh=yown[:, dc, 128 * ts:128 * ts + 128],
                                 rh=ring[:, j, 512 * gph:512 * gph + 512],
                                 st=(j == 0), sp=(j == 3): e_.matmul(
                                     o, lh, rh, start=st, stop=sp),
                                 waits=w if (gph == 0 and j == 0) else [],
                                 inc="sT" if j == 3 else None)
                mms.append(mmv)
            if grp == 0:
                e1 = P.emit("vector",
                            lambda e_, o=acc[:, ts, 0:1024], a=pB[:, 0:1024]:
                            e_.tensor_tensor(o, o, a, ALU.add),
                            waits=[("sT", mms[1])], inc="sV")
                e2 = P.emit("vector",
                            lambda e_, o=acc[:, ts, 1024:2048], a=pU[:, 0:1024]:
                            e_.tensor_tensor(o, o, a, ALU.add),
                            waits=[("sT", mms[3])], inc="sV")
            else:
                e1 = P.emit("vector",
                            lambda e_, o=acc[:, ts, 0:1024], a=pA[:, 0:1024]:
                            e_.tensor_tensor(o, o, a, ALU.add),
                            waits=[("sT", mms[3])], inc="sV")
                e2 = P.emit("vector",
                            lambda e_, o=acc[:, ts, 1024:2048], a=pA[:, 1024:2048]:
                            e_.tensor_tensor(o, o, a, ALU.add),
                            waits=[("sT", mms[3])], inc="sV")
            trk[("pj", grp)] = e2
        if qd == 2:
            load_wproj(0, [("sT", mmv)])
        elif qd == 3:
            load_wproj(1, [("sT", mmv)])
    proj_last_mm = P.cnt["sT"]
    proj_last_ev = max(trk[("pj", 0)], trk[("pj", 1)])
    trk[("dp_half", 0)] = trk[("pj", 1)]
    trk[("dp_half", 1)] = trk[("pj", 1)]
    trk["pA1"] = ("sV", trk[("pj", 1)])

    preload_gu(0, 0, [("sT", proj_last_mm)])
    preload_gu(0, 1, [])
    preload_wd(0, 0, 0, [("sT", proj_last_mm)])
    preload_wd(0, 1, 1, [])
    preload_wd(0, 2, 2, [])
    dp_wd_chain = []
    for e in range(16):
        for hg in range(4):
            if e == 0 and hg < 3:
                continue
            if e == 15 and hg == 3:
                continue
            dp_wd_chain.append((e, hg))

    for e in range(16):
        ei = e + 1
        if e >= 1:
            exw = [("sV", uw_m[(e - 1, IC - 1)])] if e >= 2 else []
            emit_wbc(e, exw)
        ex0 = []
        if e == 0:
            ex0 = [("sT", proj_last_mm), ("sV", trk[("pj", 0)]),
                   ("sV", trk[("pj", 1)])]
        gu_last = emit_gu_glue(ei, e, ex0)
        if e == 15:
            last_gu_mark = gu_last
            preload_wd(15, 3, 0, [("sT", last_gu_mark)], dest=wdx[:, :, :],
                       sem="sWdx")
        if e + 1 < 16:
            preload_gu(e + 1, 0, [("sT", gu_last)])
            preload_gu(e + 1, 1, [])
        if e >= 1:
            emit_dp(ei - 1, e - 1)

    add_m = {}

    def p5_ts(ts):
        par = ts % 2
        xo = xo5a if par == 0 else xo5b
        if ts < 2:
            w = [("sT", last_gu_mark)]
        else:
            w = [("sV", add_m[ts - 2])]
        vx = P.dma("sync",
                   lambda e, a=x_own[128 * ts:128 * ts + 128, :], b=xo[:, :]:
                   e.dma_start(out=b, in_=a),
                   waits=w, sem="sQxo%d" % par)
        wadd = [("sQxo%d" % par, vx)]
        for hg in range(4):
            wadd.append(("sV", dpev[(16, hg, ts)]))
        va = P.emit("vector",
                    lambda e, o=acc[:, ts, :], b=xo[:, :]: e.tensor_tensor(
                        o, o, b, ALU.add),
                    waits=wadd, inc="sV")
        add_m[ts] = va
        P.dma("sync",
              lambda e, o=out_own[128 * ts:128 * ts + 128, :], a=acc[:, ts, :]:
              e.dma_start(out=o, in_=a),
              waits=[("sV", va)], sem="sQst")

    for ts in range(8):
        for hg in range(4):
            apf, sn, vwd = wd_dma[(15, hg)]
            ph = (4 * ts + hg) % 2
            w = [(sn, vwd), ("sV", a2_m[(16, IC - 1)])]
            key = ("dp_half", ph)
            if trk.get(key):
                w.append(("sV", trk[key]))
            for icx in range(IC):
                mmv = P.emit("tensor",
                             lambda e_, o=pA[:, 512 * ph:512 * ph + 512],
                             lh=a2[:, a2_par(15), icx, 128 * ts:128 * ts + 128],
                             rh=apf(icx),
                             st=(icx == 0), sp=(icx == IC - 1): e_.matmul(
                                 o, lh, rh, start=st, stop=sp),
                             waits=w if icx == 0 else [],
                             inc="sT" if icx == IC - 1 else None)
            evv = P.emit("vector",
                         lambda e_, o=acc[:, ts, 512 * hg:512 * hg + 512],
                         a=pA[:, 512 * ph:512 * ph + 512]: e_.tensor_tensor(
                             o, o, a, ALU.add),
                         waits=[("sT", mmv)], inc="sV")
            dpev[(16, hg, ts)] = evv
            trk[("dp_half", ph)] = evv
        p5_ts(ts)
    all_mm = P.cnt["sT"]

    fin = P.cnt["sQst"]
    fw = [("sQst", fin)]
    if P.cnt.get("sQfin"):
        fw.append(("sQfin", P.cnt["sQfin"]))
    P.emit("sync", lambda e: e.nop(), waits=fw)

    nc._dbg = {"xnown": xnown, "wr_sb": wr_sb, "rbiasb": rbiasb, "rt": rt,
               "xnt_b": xnt_b, "lnt": lnt, "q_sb": q_sb, "k_sb": k_sb,
               "v_sb": v_sb, "acc": acc, "xo3": xo3, "wT_sb": wT_sb,
               "a2": a2, "yown": yown, "wbc_sb": wbc_sb, "xfm": xfm}

    with nc.Block() as block:
        @block.sync
        def _(eng):
            P.replay(eng, "sync")

        @block.gpsimd
        def _(eng):
            P.replay(eng, "gpsimd")

        @block.tensor
        def _(eng):
            P.replay(eng, "tensor")

        @block.scalar
        def _(eng):
            P.replay(eng, "scalar")

        @block.vector
        def _(eng):
            P.replay(eng, "vector")

    return nc


def kernel(L0_attn_x, L0_attn_w_attn, L0_attn_b_attn, L0_attn_w_proj,
           L0_attn_b_proj, L0_ffn_Wr, L0_ffn_bias, L0_ffn_Wg, L0_ffn_Wu,
           L0_ffn_Wd, L0_ffn_sWg, L0_ffn_sWu, L0_ffn_sWd):
    x = np.ascontiguousarray(np.asarray(L0_attn_x, dtype=np.float32))
    w_attn = np.asarray(L0_attn_w_attn, dtype=np.float32)
    b_attn = np.asarray(L0_attn_b_attn, dtype=np.float32)
    w_projv = np.ascontiguousarray(np.asarray(L0_attn_w_proj, dtype=np.float32))
    b_proj = np.asarray(L0_attn_b_proj, dtype=np.float32)
    Wr = np.asarray(L0_ffn_Wr, dtype=np.float32)
    rbias = np.asarray(L0_ffn_bias, dtype=np.float32)
    Wg = np.ascontiguousarray(np.asarray(L0_ffn_Wg, dtype=np.float32))
    Wu = np.ascontiguousarray(np.asarray(L0_ffn_Wu, dtype=np.float32))
    Wd = np.ascontiguousarray(np.asarray(L0_ffn_Wd, dtype=np.float32))
    sWg = np.ascontiguousarray(np.asarray(L0_ffn_sWg, dtype=np.float32))
    sWu = np.ascontiguousarray(np.asarray(L0_ffn_sWu, dtype=np.float32))
    sWd = np.ascontiguousarray(np.asarray(L0_ffn_sWd, dtype=np.float32))

    nc = build_program()

    in_maps = []
    for c in range(8):
        s, q4 = c // 4, c % 4
        wq = w_attn[:, 512 * q4:512 * (q4 + 1)]
        wk = w_attn[:, 2048 + 512 * q4:2048 + 512 * (q4 + 1)]
        wv = w_attn[:, 4096 + 512 * q4:4096 + 512 * (q4 + 1)]
        bq = b_attn[512 * q4:512 * (q4 + 1)]
        bk = b_attn[2048 + 512 * q4:2048 + 512 * (q4 + 1)]
        bv = b_attn[4096 + 512 * q4:4096 + 512 * (q4 + 1)]
        in_maps.append({
            "x_seq": np.ascontiguousarray(x[s]),
            "x_own": np.ascontiguousarray(x[s, 1024 * q4:1024 * (q4 + 1)]),
            "wqkv": np.ascontiguousarray(np.concatenate([wq, wk, wv], axis=1)),
            "bqt_in": np.ascontiguousarray(
                np.concatenate([bq.reshape(4, 128).T, bk.reshape(4, 128).T], axis=1)),
            "bv_row": np.ascontiguousarray(bv[None, :]),
            "w_proj": w_projv,
            "bproj_row": np.ascontiguousarray(b_proj[None, :]),
            "wr_t": np.ascontiguousarray(Wr.T),
            "rbias_row": np.ascontiguousarray(rbias[None, :]),
            "Wg_in": Wg, "Wu_in": Wu, "Wd_in": Wd,
            "sWg_in": sWg, "sWu_in": sWu, "sWd_in": sWd,
        })

    res = run_bass_kernel_spmd(nc, in_maps, core_ids=list(range(8)))
    out = np.empty((2, 4096, 2048), dtype=np.float32)
    for c in range(8):
        s, q4 = c // 4, c % 4
        out[s, 1024 * q4:1024 * (q4 + 1)] = res.results[c]["out_own"]
    return out
